# version 12
# baseline (speedup 1.0000x reference)
import numpy as np
import ml_dtypes
from contextlib import ExitStack

import concourse.bass as bass
import concourse.mybir as mybir
from concourse.bass_utils import run_bass_kernel_spmd

F32 = mybir.dt.float32
BF16 = mybir.dt.bfloat16
AF = mybir.ActivationFunctionType
ALU = mybir.AluOpType
AX = mybir.AxisListType.X

B, L, D = 2, 8192, 1024
H, KV, HD = 16, 4, 64
OWN = 2048
DC = 8
NE = 10
PAIRS = [(0, 4), (1, 5), (2, 6), (3, 7), (8, 12), (9, 13), (10, 14), (11, 15)]
ROPE_BASE = 10000.0
ALIBI_SKIP_T = 170.0


def _keep_chunks():
    slopes = _alibi_slopes(H)
    keep = []
    for h in range(H):
        ks = [kc for kc in range(64)
              if slopes[h] * ((L - 1) - (kc * 128 + 127)) <= ALIBI_SKIP_T]
        keep.append(sorted(ks))
    return keep


def _alibi_slopes(n_heads):
    import math
    start = 2.0 ** (-2.0 ** (-(math.log2(n_heads) - 3)))
    return [start * start ** i for i in range(n_heads)]


class Ctr:
    def __init__(self, es, nc, name):
        self.sem = es.enter_context(nc.semaphore(name))
        self.v = 0


class Rec:
    ENGS = ("tensor", "vector", "scalar", "sync", "gpsimd")

    def __init__(self, selfsems=None):
        self.q = {e: [] for e in self.ENGS}
        self.last = {}
        self.lastinfo = {}
        self.selfsems = selfsems or {}
        self.autoc = {}

    def op(self, eng, method, *a, inc=None, **kw):
        incs = []
        if inc:
            for ctr, k in inc:
                ctr.v += k
                incs.append((ctr.sem, k))
        if self.autoc.get(eng):
            self.chain(eng)
        cell = {}
        def f(e, m=method, a=a, kw=kw, incs=tuple(incs), cell=cell):
            ins = getattr(e, m)(*a, **kw)
            cell["i"] = ins
            for s, k in incs:
                ins.then_inc(s, k)
        self.q[eng].append(f)
        self.last[eng] = cell
        self.lastinfo[eng] = (inc[0][0], inc[0][0].v) if inc else None

    def inc_last(self, eng, ctr, k=1):
        ctr.v += k
        cell = self.last[eng]
        self.q[eng].append(lambda e, c=cell, s=ctr.sem, kk=k: c["i"].then_inc(s, kk))
        self.lastinfo[eng] = (ctr, ctr.v)

    def chain(self, eng):
        info = self.lastinfo.get(eng)
        if info is None:
            if eng not in self.last:
                return
            ctr = self.selfsems[eng]
            self.inc_last(eng, ctr)
            info = (ctr, ctr.v)
        ctr, v = info
        self.q[eng].append(lambda e, s=ctr.sem, n=v: e.wait_ge(s, n))
        self.lastinfo[eng] = None

    def auto_chain(self, eng, on):
        self.autoc[eng] = on

    def wait(self, eng, ctr, v=None):
        vv = ctr.v if v is None else v
        if vv <= 0:
            return
        self.q[eng].append(lambda e, s=ctr.sem, n=vv: e.wait_ge(s, n))

    def raw(self, eng, fn):
        self.q[eng].append(fn)

    def emit(self, nc):
        with nc.Block() as blk:
            for name in self.ENGS:
                ops = self.q[name]
                if not ops:
                    continue
                def body(e, ops=ops):
                    for f in ops:
                        f(e)
                getattr(blk, name)(body)


def _act_set_id(nc, set_name):
    from concourse.hw_specs import get_activation_tables
    names = list(get_activation_tables(nc.m.arch).keys())
    return names.index(set_name)


def _load_table(rec, set_id):
    def f(e, idx=set_id):
        e.add_instruction(mybir.InstLoadActFuncSet(
            name=e.bass.get_next_instruction_name(),
            act_func_set_id=idx, ins=[], outs=[]))
    rec.raw("scalar", f)


def build_nc(phases=4, nonce=0, stripes=16):
    import os as _os
    CUT = set(_os.environ.get("KMK_CUT", "").split(","))
    nc = bass.Bass("TRN2", target_bir_lowering=False)
    _lp = nc.allow_low_precision("bf16 elementwise outputs feed bf16 matmul operands by design")
    _lp.__enter__()

    xp_d = nc.dram_tensor("xb", [L, D], F32, kind="ExternalInput").ap()
    xo_d = nc.dram_tensor("xown", [OWN, D], F32, kind="ExternalInput").ap()
    wq_d = nc.dram_tensor("wqsh", [D, D], F32, kind="ExternalInput").ap()
    wk_d = nc.dram_tensor("wk", [D, 256], F32, kind="ExternalInput").ap()
    wv_d = nc.dram_tensor("wv", [D, 256], F32, kind="ExternalInput").ap()
    wo_d = nc.dram_tensor("wo", [D, D], F32, kind="ExternalInput").ap()
    gt_d = nc.dram_tensor("gt", [D, 8], F32, kind="ExternalInput").ap()
    w1_d = nc.dram_tensor("w1t", [NE, D, 1536], F32, kind="ExternalInput").ap()
    w2_d = nc.dram_tensor("w2t", [NE, 768, D], F32, kind="ExternalInput").ap()
    cosk_d = nc.dram_tensor("cosk", [128, L], BF16, kind="ExternalInput").ap()
    sink_d = nc.dram_tensor("sink", [128, L], BF16, kind="ExternalInput").ap()
    cosq_d = nc.dram_tensor("cosq", [128, OWN], BF16, kind="ExternalInput").ap()
    sinq_d = nc.dram_tensor("sinq", [128, OWN], BF16, kind="ExternalInput").ap()
    alibi_d = nc.dram_tensor("alibi", [128, 1024], F32, kind="ExternalInput").ap()
    id_d = nc.dram_tensor("ident", [128, 128], BF16, kind="ExternalInput").ap()
    sel_d = nc.dram_tensor("sel", [8, 1024], BF16, kind="ExternalInput").ap()
    out_d = nc.dram_tensor("out", [OWN, D], F32, kind="ExternalOutput").ap()
    z_d = nc.dram_tensor("zacc", [DC, 128, OWN], F32).ap()

    KEEP = _keep_chunks()
    NLE = _act_set_id(nc, "natural_log_exp_and_others")
    SIL = _act_set_id(nc, "silu_and_others")

    top = ExitStack()
    xn = top.enter_context(nc.sbuf_tensor("xn", [128, DC * OWN], BF16))
    wfm = top.enter_context(nc.sbuf_tensor("wfm", [8, OWN], F32))
    wfmb = top.enter_context(nc.sbuf_tensor("wfmb", [8, OWN], BF16))
    ones64 = top.enter_context(nc.sbuf_tensor("ones64", [65, 128], BF16))
    id_sb = top.enter_context(nc.sbuf_tensor("id_sb", [128, 128], BF16))
    alibi_sb = top.enter_context(nc.sbuf_tensor("alibi_sb", [128, 1024], F32))
    cqd = top.enter_context(nc.sbuf_tensor("cqd", [128, OWN], BF16))
    sqd = top.enter_context(nc.sbuf_tensor("sqd", [128, OWN], BF16))


    ab = ExitStack()
    K01 = ab.enter_context(nc.sbuf_tensor("K01", [128, L], BF16))
    K23 = ab.enter_context(nc.sbuf_tensor("K23", [128, L], BF16))
    Vt = [ab.enter_context(nc.sbuf_tensor(f"V{k}", [128, 64 * 65], BF16)) for k in range(KV)]

    sset = Ctr(top, nc, "sset")
    gset = Ctr(top, nc, "gset")
    sx = [Ctr(top, nc, "sx0"), Ctr(top, nc, "sx1")]
    scs = Ctr(top, nc, "scs")
    sz = [Ctr(top, nc, "sz0"), Ctr(top, nc, "sz1")]
    gw = [Ctr(top, nc, "gw0"), Ctr(top, nc, "gw1")]
    ga = [Ctr(top, nc, "ga0"), Ctr(top, nc, "ga1")]
    s4 = [Ctr(top, nc, "s4a"), Ctr(top, nc, "s4b")]
    so = [Ctr(top, nc, "so0"), Ctr(top, nc, "so1")]
    sdbg = Ctr(top, nc, "sdbg")
    all_dma = [sset, gset, scs, sdbg] + sx + sz + gw + ga + s4 + so
    T = Ctr(top, nc, "T")
    V = Ctr(top, nc, "Vv")
    S = Ctr(top, nc, "S")
    VSF = Ctr(top, nc, "VSF")
    SSF = Ctr(top, nc, "SSF")
    SELFS = {"vector": VSF, "scalar": SSF}

    a_sc = ExitStack()
    wk_sb = a_sc.enter_context(nc.sbuf_tensor("wk_sb", [128, 2048], BF16))
    wv_sb = a_sc.enter_context(nc.sbuf_tensor("wv_sb", [128, 2048], BF16))
    gt32 = a_sc.enter_context(nc.sbuf_tensor("gt32", [128, 64], F32))
    ghi = a_sc.enter_context(nc.sbuf_tensor("ghi", [128, 64], BF16))
    ghi32 = a_sc.enter_context(nc.sbuf_tensor("ghi32", [128, 64], F32))
    glo = a_sc.enter_context(nc.sbuf_tensor("glo", [128, 64], BF16))
    coskS = a_sc.enter_context(nc.sbuf_tensor("coskS", [128, 512], BF16))
    sinkS = a_sc.enter_context(nc.sbuf_tensor("sinkS", [128, 512], BF16))
    xt = a_sc.enter_context(nc.sbuf_tensor("xt", [128, 2048], F32))
    xn32 = a_sc.enter_context(nc.sbuf_tensor("xn32", [128, 1024], F32))
    xnhi32 = a_sc.enter_context(nc.sbuf_tensor("xnhi32", [128, 1024], F32))
    xntm = a_sc.enter_context(nc.sbuf_tensor("xntm", [128, 2048], BF16))
    lotm = a_sc.enter_context(nc.sbuf_tensor("lotm", [128, 2048], BF16))
    fmstripe = a_sc.enter_context(nc.sbuf_tensor("fmstripe", [128, 4096], BF16))
    kraw = a_sc.enter_context(nc.sbuf_tensor("kraw", [128, 512], BF16))
    kshuf = a_sc.enter_context(nc.sbuf_tensor("kshuf", [128, 512], BF16))
    rt1 = a_sc.enter_context(nc.sbuf_tensor("rt1", [128, 512], BF16))
    rt2 = a_sc.enter_context(nc.sbuf_tensor("rt2", [128, 512], BF16))
    vstage = a_sc.enter_context(nc.sbuf_tensor("vstage", [128, 512], BF16))
    ssum = a_sc.enter_context(nc.sbuf_tensor("ssum", [128, 64], F32))
    qsum = a_sc.enter_context(nc.sbuf_tensor("qsum", [128, 64], F32))
    e2s = a_sc.enter_context(nc.sbuf_tensor("e2s", [128, 64], F32))
    mus = a_sc.enter_context(nc.sbuf_tensor("mus", [128, 64], F32))
    vvs = a_sc.enter_context(nc.sbuf_tensor("vvs", [128, 64], F32))
    lns = a_sc.enter_context(nc.sbuf_tensor("lns", [128, 64], F32))
    rstds = a_sc.enter_context(nc.sbuf_tensor("rstds", [128, 64], F32))
    nbs = a_sc.enter_context(nc.sbuf_tensor("nbs", [128, 64], F32))
    sqscr = a_sc.enter_context(nc.sbuf_tensor("sqscr", [128, 1024], BF16))
    logits = a_sc.enter_context(nc.sbuf_tensor("logits", [8, 2048], F32))
    id32 = a_sc.enter_context(nc.sbuf_tensor("id32", [128, 128], F32))
    eps_sb = a_sc.enter_context(nc.sbuf_tensor("eps_sb", [128, 1], F32))
    ltm = a_sc.enter_context(nc.sbuf_tensor("ltm", [128, 128], F32))
    eL = a_sc.enter_context(nc.sbuf_tensor("eL", [128, 128], F32))
    probs = a_sc.enter_context(nc.sbuf_tensor("probs", [128, 128], F32))
    maskd = a_sc.enter_context(nc.sbuf_tensor("maskd", [128, 128], F32))
    scr8 = a_sc.enter_context(nc.sbuf_tensor("scr8", [128, 8], F32))
    m1p = a_sc.enter_context(nc.sbuf_tensor("m1p", [128, 16], F32))
    m2p = a_sc.enter_context(nc.sbuf_tensor("m2p", [128, 16], F32))
    rsum = a_sc.enter_context(nc.sbuf_tensor("rsum", [128, 16], F32))
    rrec = a_sc.enter_context(nc.sbuf_tensor("rrec", [128, 16], F32))
    TP = a_sc.enter_context(nc.psum_tensor("TP", [128, 1024], F32))
    TP2 = a_sc.enter_context(nc.psum_tensor("TP2", [128, 1024], F32))
    KP = a_sc.enter_context(nc.psum_tensor("KP", [128, 512], F32))
    VP = a_sc.enter_context(nc.psum_tensor("VP", [128, 512], F32))
    VT = a_sc.enter_context(nc.psum_tensor("VT", [128, 512], F32))
    GP = a_sc.enter_context(nc.psum_tensor("GP", [8, 512], F32))

    r = Rec(SELFS)
    r.auto_chain("vector", True)
    r.auto_chain("scalar", True)
    _load_table(r, NLE)
    for _ in range(nonce):
        r.op("tensor", "nop")

    r.op("sync", "dma_start", alibi_sb[:, :], alibi_d, inc=[(sset, 16)])
    r.op("sync", "dma_start", id_sb[:, :], id_d, inc=[(sset, 16)])
    r.op("sync", "dma_start", cqd[:, :], cosq_d, inc=[(sset, 16)])
    r.op("sync", "dma_start", sqd[:, :], sinq_d, inc=[(sset, 16)])
    r.op("sync", "dma_start", gt32[:, :].rearrange("p (dc e) -> p dc e", dc=8),
         gt_d.rearrange("(dc p) e -> p dc e", p=128), inc=[(sset, 16)])
    ds_setup = sset.v
    r.op("gpsimd", "dma_start", wk_sb[:, :].rearrange("p (dc c) -> p dc c", dc=8),
         wk_d.rearrange("(dc p) c -> p dc c", p=128), inc=[(gset, 16)])
    r.op("gpsimd", "dma_start", wv_sb[:, :].rearrange("p (dc c) -> p dc c", dc=8),
         wv_d.rearrange("(dc p) c -> p dc c", p=128), inc=[(gset, 16)])
    dg_w0 = gset.v

    r.op("vector", "memset", ones64[64:65, :], 1.0)
    for k in range(KV):
        r.op("vector", "memset",
             Vt[k][:, :].rearrange("p (kc c) -> p kc c", c=65)[:, :, 64:65], 1.0)
    r.wait("vector", sset, ds_setup)
    r.op("vector", "tensor_copy", id32[:, :], id_sb[:, :])
    r.op("vector", "memset", eps_sb[:, :], 1e-5)
    r.op("vector", "tensor_copy", ghi[:, :], gt32[:, :])
    r.op("vector", "tensor_copy", ghi32[:, :], ghi[:, :])
    r.op("vector", "tensor_sub", glo[:, :], gt32[:, :], ghi32[:, :], inc=[(V, 1)])
    v_ginit = V.v

    xp_r = xp_d.rearrange("(t p) c -> p t c", p=128)
    xo_r = xo_d.rearrange("(t p) c -> p t c", p=128)
    v_xt_done = {}
    s_xt_done = {}
    t_xntm_done = {}
    t_lotm_done = {}
    v_cos_done = {}
    t_kv_done = 0
    t_gate_prev = 0
    v_logit_evac = 0
    v_kevac = 0

    for s in range(stripes + 4):
        own = s >= stripes
        sow = s - stripes
        src_r = xo_r if own else xp_r
        fm_guard = max(t_kv_done, t_gate_prev)
        cb = 0
        for i in range(4):
            tau = 4 * s + i
            lt = (4 * sow + i) if own else tau
            tb = (tau % 2) * 1024
            if tau >= 2:
                r.wait("sync", V, v_xt_done[tau - 2])
                r.wait("sync", S, s_xt_done[tau - 2])
            sxp = sx[tau % 2]
            r.op("sync", "dma_start", xt[:, tb:tb + 1024], src_r[:, lt, :], inc=[(sxp, 16)])
            ds_tile = sxp.v

            r.wait("vector", sxp, ds_tile)
            r.op("vector", "reduce_sum", ssum[:, tau % 64:tau % 64 + 1], xt[:, tb:tb + 1024], axis=AX)
            r.wait("scalar", sxp, ds_tile)
            r.op("scalar", "activation", sqscr[:, :], xt[:, tb:tb + 1024], AF.Square,
                 accum_out=qsum[:, tau % 64:tau % 64 + 1], inc=[(S, 1)])
            s_sq = S.v
            s_xt_done[tau] = s_sq
            c = tau % 64
            r.wait("vector", S, s_sq)
            r.op("vector", "tensor_single_scalar", mus[:, c:c + 1], ssum[:, c:c + 1],
                 1.0 / 1024.0, ALU.mult)
            r.op("vector", "tensor_scalar", vvs[:, c:c + 1], mus[:, c:c + 1],
                 mus[:, c:c + 1], None, ALU.mult)
            r.op("vector", "scalar_tensor_tensor", vvs[:, c:c + 1], qsum[:, c:c + 1],
                 -1.0 / 1024.0, vvs[:, c:c + 1], ALU.mult, ALU.add, inc=[(V, 1)])
            v_vv = V.v
            r.wait("scalar", V, v_vv)
            r.op("scalar", "activation", lns[:, c:c + 1], vvs[:, c:c + 1], AF.Ln,
                 bias=eps_sb[:, 0:1], scale=-1.0)
            r.op("scalar", "activation", rstds[:, c:c + 1], lns[:, c:c + 1], AF.Exp,
                 scale=-0.5, inc=[(S, 1)])
            s_rstd = S.v
            r.wait("vector", S, s_rstd)
            if tau >= 2:
                r.wait("vector", T, t_xntm_done[tau - 2])
            if own:
                r.op("vector", "tensor_scalar", xn32[:, :], xt[:, tb:tb + 1024],
                     mus[:, c:c + 1], rstds[:, c:c + 1], ALU.subtract, ALU.mult)
                r.op("vector", "tensor_copy", xntm[:, tb:tb + 1024], xn32[:, :])
                if tau >= 2 and (tau - 2) in t_lotm_done:
                    r.wait("vector", T, t_lotm_done[tau - 2])
                r.op("vector", "tensor_copy", xnhi32[:, :], xntm[:, tb:tb + 1024])
                r.op("vector", "tensor_sub", lotm[:, tb:tb + 1024], xn32[:, :],
                     xnhi32[:, :], inc=[(V, 1)])
            else:
                r.op("vector", "tensor_scalar", xntm[:, tb:tb + 1024], xt[:, tb:tb + 1024],
                     mus[:, c:c + 1], rstds[:, c:c + 1], ALU.subtract, ALU.mult,
                     inc=[(V, 1)])
            v_xt_done[tau] = V.v
            v_aff = V.v

            r.wait("tensor", V, v_aff)
            TPb = TP if (tau % 2) == 0 else TP2
            for dc in range(DC):
                r.op("tensor", "matmul", TPb[:, dc * 128:dc * 128 + 128],
                     xntm[:, tb + dc * 128:tb + dc * 128 + 128], id_sb[:, :],
                     start=(dc % 4 == 0), stop=(dc % 4 == 3), skip_group_check=True)
            r.inc_last("tensor", T)
            t_hi = T.v
            r.wait("vector", T, t_hi)
            if i == 0:
                r.wait("vector", T, fm_guard)
            if own:
                dst = xn[:, :].rearrange("p (dc t) -> p dc t", dc=DC)[:, :, (4 * sow + i) * 128:(4 * sow + i) * 128 + 128]
            else:
                dst = fmstripe[:, :].rearrange("p (dc t) -> p dc t", dc=DC)[:, :, i * 128:i * 128 + 128]
            r.op("vector", "tensor_copy", dst,
                 TPb[:, :].rearrange("p (dc t) -> p dc t", dc=DC), inc=[(V, 1)])
            v_hi_evac = V.v
            if own:
                r.wait("tensor", V, v_hi_evac)
                for dc in range(DC):
                    r.op("tensor", "matmul", TPb[:, dc * 128:dc * 128 + 128],
                         lotm[:, tb + dc * 128:tb + dc * 128 + 128], id_sb[:, :],
                         start=(dc % 4 == 0), stop=(dc % 4 == 3), skip_group_check=True)
                r.inc_last("tensor", T)
                t_lotm_done[tau] = T.v
                t_xntm_done[tau] = T.v
                r.wait("vector", T, T.v)
                r.op("vector", "tensor_copy",
                     fmstripe[:, :].rearrange("p (dc t) -> p dc t", dc=DC)[:, :, i * 128:i * 128 + 128],
                     TPb[:, :].rearrange("p (dc t) -> p dc t", dc=DC), inc=[(V, 1)])
            else:
                t_xntm_done[tau] = t_hi
        v_stripe = V.v
        if own:
            pass
        else:
            if s >= 1:
                r.wait("sync", V, v_cos_done[s - 1])
            r.op("sync", "dma_start", coskS[:, cb:cb + 512], cosk_d[:, s * 512:(s + 1) * 512],
                 inc=[(scs, 16)])
            r.op("sync", "dma_start", sinkS[:, cb:cb + 512], sink_d[:, s * 512:(s + 1) * 512],
                 inc=[(scs, 16)])
            ds_cos = scs.v

        def xn_src(dc, st=s):
            return fmstripe[:, dc * 512: dc * 512 + 512]

        if "kv" in CUT or own:
            if own and "gate" not in CUT:
                r.wait("tensor", V, v_ginit)
                if v_logit_evac:
                    r.wait("tensor", V, v_logit_evac)
                first = True
                for gsrc, gw_ in ((0, ghi), (0, glo), (1, ghi)):
                    for dc in range(DC):
                        if gsrc == 0:
                            rhs = xn[:, dc * OWN + sow * 512: dc * OWN + sow * 512 + 512]
                        else:
                            rhs = fmstripe[:, dc * 512: dc * 512 + 512]
                        r.op("tensor", "matmul", GP[:, :], gw_[:, dc * 8:dc * 8 + 8], rhs,
                             start=first, stop=(gsrc == 1 and dc == DC - 1))
                        first = False
                r.inc_last("tensor", T)
                t_gate_prev = T.v
                r.wait("vector", T, t_gate_prev)
                r.op("vector", "tensor_copy", logits[:, sow * 512:(sow + 1) * 512], GP[:, :],
                     inc=[(V, 1)])
                v_logit_evac = V.v
            t_kv_done = t_gate_prev if own else t_kv_done
            continue
        if s == 0:
            r.wait("tensor", gset, dg_w0)
        r.wait("tensor", V, v_stripe)
        for cch in range(1 if "cch1" in CUT else 2):
            Kdst = K01 if cch == 0 else K23
            if v_kevac:
                r.wait("tensor", V, v_kevac)
            for dc in range(DC):
                r.op("tensor", "matmul", KP[:, :],
                     wk_sb[:, dc * 256 + cch * 128: dc * 256 + cch * 128 + 128],
                     xn_src(dc), start=(dc == 0), stop=(dc == DC - 1))
            r.inc_last("tensor", T)
            t_k = T.v
            r.wait("vector", T, t_k)
            r.op("vector", "tensor_copy", kraw[:, :], KP[:, :], inc=[(V, 1)])
            if "rope" in CUT:
                r.op("vector", "tensor_copy", Kdst[:, s * 512:(s + 1) * 512], kraw[:, :])
                skiprope = True
            else:
                skiprope = False
            if not skiprope:
                r.op("vector", "tensor_copy", kshuf[0:32, :], kraw[32:64, :])
                r.op("vector", "tensor_copy", kshuf[32:64, :], kraw[0:32, :])
                r.op("vector", "tensor_copy", kshuf[64:96, :], kraw[96:128, :])
                r.op("vector", "tensor_copy", kshuf[96:128, :], kraw[64:96, :])
                r.wait("vector", scs, ds_cos)
                r.op("vector", "tensor_mul", rt1[:, :], kraw[:, :], coskS[:, cb:cb + 512])
                r.op("vector", "tensor_mul", rt2[:, :], kshuf[:, :], sinkS[:, cb:cb + 512])
                r.op("vector", "tensor_add", Kdst[:, s * 512:(s + 1) * 512], rt1[:, :], rt2[:, :])
            for dc in range(DC):
                r.op("tensor", "matmul", VP[:, :],
                     wv_sb[:, dc * 256 + cch * 128: dc * 256 + cch * 128 + 128],
                     xn_src(dc), start=(dc == 0), stop=(dc == DC - 1))
            r.inc_last("tensor", T)
            t_v = T.v
            r.wait("vector", T, t_v)
            r.op("vector", "tensor_copy", vstage[:, :], VP[:, :], inc=[(V, 1)])
            v_vstage = V.v
            if "vtr" in CUT:
                v_kevac = V.v
                continue
            r.wait("tensor", V, v_vstage)
            for kvl in range(2):
                if kvl == 1:
                    r.wait("tensor", V, V.v)
                for tb4 in range(4):
                    r.op("tensor", "matmul", VT[:, tb4 * 64:tb4 * 64 + 64],
                         vstage[kvl * 64:(kvl + 1) * 64, tb4 * 128:(tb4 + 1) * 128],
                         id_sb[kvl * 64:(kvl + 1) * 64, kvl * 64:kvl * 64 + 64],
                         start=(tb4 == 0), stop=(tb4 == 3), skip_group_check=True)
                r.inc_last("tensor", T)
                r.wait("vector", T, T.v)
                kvi = cch * 2 + kvl
                src = VT[:, 0:256].rearrange("p (t c) -> p t c", t=4)
                dst = Vt[kvi][:, :].rearrange("p (kc c) -> p kc c", c=65)[:, s * 4:s * 4 + 4, 0:64]
                r.op("vector", "tensor_copy", dst, src, inc=[(V, 1)])
            v_kevac = V.v
        v_cos_done[s] = V.v
        t_kv_done = T.v


    if stripes < 4:
        _dbg_out(r, nc, out_d, xn, sdbg, (T, V, S), all_dma)
        r.emit(nc)
        a_sc.close()
        ab.close(); top.close(); _lp.__exit__(None, None, None)
        return nc
    r.wait("tensor", V, V.v)
    for tb in range(16):
        r.op("tensor", "matmul", TP[:, tb * 8:tb * 8 + 8],
             logits[0:8, tb * 128:(tb + 1) * 128], id32[0:8, 0:8],
             start=(tb == 0), stop=(tb == 15), skip_group_check=True)
    r.inc_last("tensor", T)
    t_ltm = T.v
    r.wait("vector", T, t_ltm)
    r.op("vector", "tensor_copy", ltm[:, :], TP[:, 0:128], inc=[(V, 1)])
    r.wait("scalar", V, V.v)
    r.op("scalar", "activation", eL[:, :], ltm[:, :], AF.Exp, inc=[(S, 1)])
    r.wait("vector", S, S.v)
    for tb in range(16):
        blk = slice(tb * 8, tb * 8 + 8)
        r.op("vector", "reduce_sum", rsum[:, tb:tb + 1], eL[:, blk], axis=AX)
    r.op("vector", "reciprocal", rrec[:, 0:16], rsum[:, 0:16])
    for tb in range(16):
        blk = slice(tb * 8, tb * 8 + 8)
        r.op("vector", "tensor_single_scalar", probs[:, blk], eL[:, blk],
             rrec[:, tb:tb + 1], ALU.mult)
        r.op("vector", "reduce_max", m1p[:, tb:tb + 1], probs[:, blk], axis=AX)
        r.op("vector", "tensor_single_scalar", scr8[:, 0:8], probs[:, blk],
             m1p[:, tb:tb + 1], ALU.is_ge)
        r.op("vector", "scalar_tensor_tensor", maskd[:, blk], scr8[:, 0:8], 2.0,
             probs[:, blk], ALU.mult, ALU.subtract)
        r.op("vector", "tensor_reduce", m2p[:, tb:tb + 1], maskd[:, blk], AX, ALU.min,
             negate=True)
        r.op("vector", "tensor_single_scalar", scr8[:, 0:8], probs[:, blk],
             m2p[:, tb:tb + 1], ALU.is_ge)
        r.op("vector", "tensor_mul", maskd[:, blk], probs[:, blk], scr8[:, 0:8],
             inc=[(V, 1)])
    v_wtm = V.v
    v_gp_evac = 0
    for rnd in range(4):
        r.wait("tensor", V, v_wtm if rnd == 0 else v_gp_evac)
        for q4 in range(4):
            tb = rnd * 4 + q4
            r.op("tensor", "matmul", GP[:, q4 * 128:(q4 + 1) * 128],
                 maskd[:, tb * 8:tb * 8 + 8], id32[:, :],
                 start=(q4 == 0), stop=(q4 == 3), skip_group_check=True)
        r.inc_last("tensor", T)
        r.wait("vector", T, T.v)
        r.op("vector", "tensor_copy", wfm[:, rnd * 512:(rnd + 1) * 512], GP[:, :],
             inc=[(V, 1)])
        v_gp_evac = V.v
    r.op("vector", "tensor_copy", wfmb[:, :], wfm[:, :], inc=[(V, 1)])

    if phases == 15:
        for ctr in (T, V, S):
            r.wait("gpsimd", ctr, ctr.v)
        r.op("gpsimd", "dma_start",
             out_d[0:16, :].rearrange("(a b) c -> a (b c)", b=2),
             wfm[:, :], inc=[(sdbg, 16)])
        r.op("gpsimd", "dma_start",
             out_d[16:32, :].rearrange("(a b) c -> a (b c)", b=2),
             logits[:, :], inc=[(sdbg, 16)])
        r.op("gpsimd", "dma_start",
             out_d[32:48, :].rearrange("a (d c) -> (a d) c", d=8),
             probs[:, :], inc=[(sdbg, 16)])
        r.op("gpsimd", "dma_start",
             out_d[48:64, :].rearrange("a (d c) -> (a d) c", d=8),
             ltm[:, :], inc=[(sdbg, 16)])
        r.op("gpsimd", "dma_start",
             out_d[64:72, :].rearrange("a (d c) -> (a d) c", d=16),
             ghi32[:, :], inc=[(sdbg, 16)])
        r.op("gpsimd", "dma_start",
             out_d[72:80, :].rearrange("a (d c) -> (a d) c", d=16),
             glo[:, :], inc=[(sdbg, 16)])
        r.op("gpsimd", "dma_start",
             out_d[80:88, :].rearrange("a (d c) -> (a d) c", d=16),
             gt32[:, :], inc=[(sdbg, 16)])
        for ctr in all_dma:
            r.wait("gpsimd", ctr, ctr.v)
        r.emit(nc)
        a_sc.close(); ab.close(); top.close(); _lp.__exit__(None, None, None)
        return nc
    if phases <= 1:
        _dbg_out(r, nc, out_d, xn, sdbg, (T, V, S), all_dma)
    r.emit(nc)
    a_sc.close()
    if phases <= 1:
        ab.close(); top.close(); _lp.__exit__(None, None, None)
        return nc

    b_sc = ExitStack()
    wq_sb = b_sc.enter_context(nc.sbuf_tensor("wq_sb", [128, 8192], BF16))
    wo_sb = b_sc.enter_context(nc.sbuf_tensor("wo_sb", [128, 8192], BF16))
    prewo = b_sc.enter_context(nc.sbuf_tensor("prewo", [128, 8192], BF16))
    PT_A = b_sc.enter_context(nc.sbuf_tensor("PT_A", [128, 3072], BF16))
    PT_B = b_sc.enter_context(nc.sbuf_tensor("PT_B", [128, 3072], BF16))
    qraw = b_sc.enter_context(nc.sbuf_tensor("qraw", [128, 1024], BF16))
    qshuf = b_sc.enter_context(nc.sbuf_tensor("qshuf", [128, 1024], BF16))
    Qpair = b_sc.enter_context(nc.sbuf_tensor("Qpair", [128, 1024], BF16))
    rt1b = b_sc.enter_context(nc.sbuf_tensor("rt1b", [128, 1024], BF16))
    rb_sb = b_sc.enter_context(nc.sbuf_tensor("rb_sb", [64, 512], F32))
    rlb = b_sc.enter_context(nc.sbuf_tensor("rlb", [65, 2048], BF16))
    rtmp = b_sc.enter_context(nc.sbuf_tensor("rtmp", [65, 1024], F32))
    zst = b_sc.enter_context(nc.sbuf_tensor("zst", [128, 2048], F32))
    PS_A = b_sc.enter_context(nc.psum_tensor("PS_A", [128, 1024], F32))
    PS_B = b_sc.enter_context(nc.psum_tensor("PS_B", [128, 1024], F32))
    PVA = b_sc.enter_context(nc.psum_tensor("PVA", [128, 1024], F32))
    PVB = b_sc.enter_context(nc.psum_tensor("PVB", [128, 1024], F32))

    r = Rec(SELFS)
    r.auto_chain("vector", True)
    r.op("gpsimd", "dma_start", wq_sb[:, :].rearrange("p (dc c) -> p dc c", dc=8),
         wq_d.rearrange("(dc p) c -> p dc c", p=128), inc=[(gset, 16)])
    r.op("gpsimd", "dma_start", wo_sb[:, :].rearrange("p (dc c) -> p dc c", dc=8),
         wo_d.rearrange("(dc p) c -> p dc c", p=128), inc=[(gset, 16)])
    dg_wqo = gset.v
    r.wait("tensor", sset, ds_setup)
    r.wait("tensor", gset, dg_wqo)

    v_prewo_done = 0
    t_pv_prev = 0
    ds_zw = {}
    z_writes = 0
    for qsh in range(2):
        qo = qsh * 1024
        def emit_qbuild(p, gate_s=None, gate_t=None):
            if gate_s is not None:
                r.wait("tensor", S, gate_s)
            if gate_t is not None:
                r.wait("tensor", T, gate_t)
            if v_prewo_done and gate_s is None:
                r.wait("tensor", V, v_prewo_done)
            for st in range(2):
                for dc in range(DC):
                    r.op("tensor", "matmul", PS_A[:, st * 512:(st + 1) * 512],
                         wq_sb[:, dc * 1024 + p * 128: dc * 1024 + p * 128 + 128],
                         xn[:, dc * OWN + qo + st * 512: dc * OWN + qo + st * 512 + 512],
                         start=(dc == 0), stop=(dc == DC - 1))
            r.inc_last("tensor", T)
            t_q = T.v
            r.wait("vector", T, t_q)
            r.op("vector", "tensor_copy", qraw[:, :], PS_A[:, :])
            r.op("vector", "tensor_copy", qshuf[0:32, :], qraw[32:64, :])
            r.op("vector", "tensor_copy", qshuf[32:64, :], qraw[0:32, :])
            r.op("vector", "tensor_copy", qshuf[64:96, :], qraw[96:128, :])
            r.op("vector", "tensor_copy", qshuf[96:128, :], qraw[64:96, :])
            r.op("vector", "tensor_mul", rt1b[:, :], qraw[:, :], cqd[:, qo:qo + 1024])
            r.op("vector", "tensor_mul", qraw[:, :], qshuf[:, :], sqd[:, qo:qo + 1024])
            r.op("vector", "tensor_add", Qpair[:, :], rt1b[:, :], qraw[:, :],
                 inc=[(V, 1)])
            return V.v

        v_qrope = emit_qbuild(0)
        for p in range(8):
            hA, hB = PAIRS[p]
            Ksrc = K01 if p < 4 else K23
            VA = Vt[hA // 4]
            VB = Vt[hB // 4]
            sA = {}
            sB = {}
            t_pv = {}
            keepA = KEEP[hA]
            keepB = KEEP[hB]
            kcs = keepB
            r.wait("tensor", V, v_qrope)

            def emit_pv(it):
                kc = kcs[it]
                buf = (it % 3) * 1024
                if it == 0 and v_prewo_done:
                    r.wait("tensor", V, v_prewo_done)
                r.wait("tensor", S, sB[kc])
                if kc in sA:
                    for st in range(2):
                        r.op("tensor", "matmul", PVA[0:65, st * 512:(st + 1) * 512],
                             VA[:, kc * 65: kc * 65 + 65],
                             PT_A[:, buf + st * 512: buf + st * 512 + 512],
                             start=(kc == keepA[0]), stop=(kc == 63),
                             skip_group_check=True)
                for st in range(2):
                    r.op("tensor", "matmul", PVB[0:65, st * 512:(st + 1) * 512],
                         VB[:, kc * 65: kc * 65 + 65],
                         PT_B[:, buf + st * 512: buf + st * 512 + 512],
                         start=(kc == keepB[0]), stop=(kc == 63),
                         skip_group_check=True)
                r.inc_last("tensor", T)
                t_pv[it] = T.v

            prevA = None
            for it, kc in enumerate(kcs):
                doA = kc in keepA
                if doA:
                    if prevA is not None:
                        r.wait("tensor", S, sA[prevA])
                    for st in range(2):
                        r.op("tensor", "matmul", PS_A[:, st * 512:(st + 1) * 512],
                             Ksrc[0:64, kc * 128:(kc + 1) * 128],
                             Qpair[0:64, st * 512:(st + 1) * 512], start=True, stop=True)
                    r.inc_last("tensor", T)
                    t_qkA = T.v
                if it > 0:
                    r.wait("tensor", S, sB[kcs[it - 1]])
                for st in range(2):
                    r.op("tensor", "matmul", PS_B[:, st * 512:(st + 1) * 512],
                         Ksrc[64:128, kc * 128:(kc + 1) * 128],
                         Qpair[64:128, st * 512:(st + 1) * 512], start=True, stop=True)
                r.inc_last("tensor", T)
                t_qkB = T.v
                buf = (it % 3) * 1024
                if doA:
                    r.wait("scalar", T, t_qkA)
                    if it >= 3:
                        r.wait("scalar", T, t_pv[it - 3])
                    elif t_pv_prev:
                        r.wait("scalar", T, t_pv_prev)
                    r.op("scalar", "activation", PT_A[:, buf:buf + 1024], PS_A[:, :], AF.Exp,
                         bias=alibi_sb[:, hA * 64 + kc: hA * 64 + kc + 1], inc=[(S, 1)])
                    sA[kc] = S.v
                    prevA = kc
                r.wait("scalar", T, t_qkB)
                if it >= 3:
                    r.wait("scalar", T, t_pv[it - 3])
                elif t_pv_prev:
                    r.wait("scalar", T, t_pv_prev)
                r.op("scalar", "activation", PT_B[:, buf:buf + 1024], PS_B[:, :], AF.Exp,
                     bias=alibi_sb[:, hB * 64 + kc: hB * 64 + kc + 1], inc=[(S, 1)])
                sB[kc] = S.v
                if it >= 2:
                    emit_pv(it - 2)
            for it2 in range(max(0, len(kcs) - 2), len(kcs)):
                emit_pv(it2)
            t_qk63_prev = t_qkB
            r.wait("scalar", T, t_pv[len(kcs) - 1])
            r.op("scalar", "activation", rtmp[64:65, 0:1024], PVA[64:65, :], AF.Ln,
                 inc=[(S, 1)])
            r.chain("scalar")
            r.op("scalar", "activation", rlb[64:65, 0:1024], rtmp[64:65, 0:1024],
                 AF.Exp, scale=-1.0, inc=[(S, 1)])
            r.chain("scalar")
            r.op("scalar", "activation", rtmp[64:65, 0:1024], PVB[64:65, :], AF.Ln,
                 inc=[(S, 1)])
            r.chain("scalar")
            r.op("scalar", "activation", rlb[64:65, 1024:2048], rtmp[64:65, 0:1024],
                 AF.Exp, scale=-1.0, inc=[(S, 1)])
            s_rec = S.v
            if p < 7:
                v_qrope = emit_qbuild(p + 1, gate_s=sB[63], gate_t=t_qkB)
            r.wait("tensor", S, s_rec)
            for head, PVx, hh in ((0, PVA, hA), (1, PVB, hB)):
                if head == 1:
                    r.wait("tensor", V, V.v)
                for st in range(2):
                    r.op("tensor", "matmul", PS_B[0:64, st * 512:(st + 1) * 512],
                         ones64[64:65, 0:64],
                         rlb[64:65, head * 1024 + st * 512: head * 1024 + (st + 1) * 512],
                         start=True, stop=True)
                r.inc_last("tensor", T)
                t_bc = T.v
                r.wait("vector", T, t_bc)
                ro = (hh % 2) * 64
                co = (hh // 2) * 1024
                for st in range(2):
                    r.op("vector", "tensor_copy", rb_sb[:, :],
                         PS_B[0:64, st * 512:(st + 1) * 512])
                    r.op("vector", "tensor_mul",
                         prewo[ro:ro + 64, co + st * 512:co + (st + 1) * 512],
                         PVx[0:64, st * 512:(st + 1) * 512], rb_sb[:, :],
                         inc=[(V, 1)])
            v_prewo_done = V.v
            r.wait("tensor", V, v_prewo_done)
            t_pv_prev = t_pv[len(kcs) - 1]

        r.wait("tensor", V, v_prewo_done)
        for oc in range(DC):
            PSx = PS_A if (oc % 2 == 0) else PS_B
            for st in range(2):
                for ic in range(DC):
                    r.op("tensor", "matmul", PSx[:, st * 512:(st + 1) * 512],
                         wo_sb[:, ic * 1024 + oc * 128: ic * 1024 + oc * 128 + 128],
                         prewo[:, ic * 1024 + st * 512: ic * 1024 + st * 512 + 512],
                         start=(ic == 0), stop=(ic == DC - 1))
            r.inc_last("tensor", T)
            t_wo = T.v
            zb = (oc % 2) * 1024
            r.wait("vector", T, t_wo)
            szp = sz[z_writes % 2]
            if z_writes >= 2:
                r.wait("vector", sz[z_writes % 2], ds_zw[z_writes - 2])
            r.op("vector", "tensor_copy", zst[:, zb:zb + 1024], PSx[:, :], inc=[(V, 1)])
            v_ev = V.v
            r.wait("sync", V, v_ev)
            r.op("sync", "dma_start", z_d[oc, :, qo:qo + 1024], zst[:, zb:zb + 1024],
                 inc=[(szp, 16)])
            ds_zw[z_writes] = szp.v
            z_writes += 1
            if oc + 2 < DC:
                r.wait("tensor", V, v_ev)
        v_prewo_done = V.v
    sz_fin = (sz[0].v, sz[1].v)

    if phases <= 2:
        _dbg_z(r, nc, out_d, z_d, sdbg, (T, V, S), all_dma)
    r.emit(nc)
    b_sc.close()
    ab.close()
    if phases <= 2:
        top.close(); _lp.__exit__(None, None, None)
        return nc

    c_sc = ExitStack()
    sel_sb = c_sc.enter_context(nc.sbuf_tensor("sel_sb", [8, 1024], BF16))
    w1sb = c_sc.enter_context(nc.sbuf_tensor("w1sb", [128, 2 * 12288], BF16))
    w2sb = c_sc.enter_context(nc.sbuf_tensor("w2sb", [128, 2 * 6144], BF16))
    Abuf = c_sc.enter_context(nc.sbuf_tensor("Abuf", [128, 6 * 2048], BF16))
    gact = c_sc.enter_context(nc.sbuf_tensor("gact", [128, 2048], F32))
    wbc = c_sc.enter_context(nc.sbuf_tensor("wbc", [128, 2048], BF16))
    atmp = c_sc.enter_context(nc.sbuf_tensor("atmp", [128, 1024], BF16))
    zst3 = c_sc.enter_context(nc.sbuf_tensor("zst3", [128, 4096], F32))
    PA1 = c_sc.enter_context(nc.psum_tensor("PA1", [128, 1024], F32))
    PA2 = c_sc.enter_context(nc.psum_tensor("PA2", [128, 1024], F32))
    PB1 = c_sc.enter_context(nc.psum_tensor("PB1", [128, 1024], F32))
    PB2 = c_sc.enter_context(nc.psum_tensor("PB2", [128, 1024], F32))

    r = Rec(SELFS)
    r.auto_chain("vector", True)
    r.auto_chain("scalar", True)
    _load_table(r, SIL)
    r.op("gpsimd", "dma_start", sel_sb[:, :], sel_d, inc=[(gset, 16)])
    gset_sel = gset.v
    r.wait("gpsimd", sz[0], sz_fin[0])
    r.wait("gpsimd", sz[1], sz_fin[1])

    dg_w = {}
    for e in range(2):
        buf = e % 2
        r.op("gpsimd", "dma_start",
             w1sb[:, buf * 12288:(buf + 1) * 12288].rearrange("p (dc f) -> p dc f", dc=DC),
             w1_d[e].rearrange("(dc p) f -> p dc f", p=128), inc=[(gw[buf], 16)])
        r.op("gpsimd", "dma_start",
             w2sb[:, buf * 6144:(buf + 1) * 6144].rearrange("p (hc c) -> p hc c", hc=6),
             w2_d[e].rearrange("(hc p) c -> p hc c", p=128), inc=[(gw[buf], 16)])
        dg_w[e] = gw[buf].v

    v_A_done = {}
    v_z3evac = {}
    dg_acc = {}
    t_h1_done = {}
    t_eo_done = {}
    v_last_eoevac = 0
    for e in range(NE):
        buf = e % 2
        r.wait("tensor", gw[buf], dg_w[e])
        if v_last_eoevac:
            r.wait("tensor", V, v_last_eoevac)
        if e < 8:
            if e == 0:
                r.wait("tensor", gset, gset_sel)
            for st in range(4):
                Px = (PA1, PA2, PB1, PB2)[st]
                r.op("tensor", "matmul", Px[:, 0:512],
                     sel_sb[0:8, e * 128:(e + 1) * 128], wfmb[0:8, st * 512:(st + 1) * 512],
                     start=True, stop=True)
            r.inc_last("tensor", T)
            t_wb = T.v
            r.wait("vector", T, t_wb)
            for st in range(4):
                Px = (PA1, PA2, PB1, PB2)[st]
                r.op("vector", "tensor_copy", wbc[:, st * 512:(st + 1) * 512],
                     Px[:, 0:512], inc=[(V, 1)])
            v_wbc = V.v
            r.wait("tensor", V, v_wbc)
        for j in range(6):
            for half in range(2):
                P1 = PA1 if half == 0 else PB1
                P2 = PA2 if half == 0 else PB2
                if (j - 1, half) in v_A_done:
                    r.wait("tensor", V, v_A_done[(j - 1, half)])
                for fcj, Px in ((j, P1), (j + 6, P2)):
                    for st in range(2):
                        for dc in range(DC):
                            r.op("tensor", "matmul", Px[:, st * 512:(st + 1) * 512],
                                 w1sb[:, buf * 12288 + dc * 1536 + fcj * 128:
                                      buf * 12288 + dc * 1536 + fcj * 128 + 128],
                                 xn[:, dc * OWN + half * 1024 + st * 512:
                                    dc * OWN + half * 1024 + st * 512 + 512],
                                 start=(dc == 0), stop=(dc == DC - 1))
                r.inc_last("tensor", T)
                t_h1 = T.v
                r.wait("scalar", T, t_h1)
                r.op("scalar", "activation", gact[:, half * 1024:(half + 1) * 1024],
                     P2[:, :], AF.Silu, inc=[(S, 1)])
                s_si = S.v
                r.wait("vector", S, s_si)
                acol = j * 2048 + half * 1024
                if e < 8:
                    r.op("vector", "tensor_mul", atmp[:, :], P1[:, :],
                         gact[:, half * 1024:(half + 1) * 1024])
                    r.op("vector", "tensor_mul", Abuf[:, acol:acol + 1024], atmp[:, :],
                         wbc[:, half * 1024:(half + 1) * 1024], inc=[(V, 1)])
                else:
                    r.op("vector", "tensor_mul", Abuf[:, acol:acol + 1024], P1[:, :],
                         gact[:, half * 1024:(half + 1) * 1024], inc=[(V, 1)])
                v_A_done[(j, half)] = V.v
        v_A_done.pop((-1, 0), None)
        t_h1_done[e] = T.v
        if e + 2 < NE:
            r.wait("gpsimd", T, t_h1_done[e])
            r.op("gpsimd", "dma_start",
                 w1sb[:, buf * 12288:(buf + 1) * 12288].rearrange("p (dc f) -> p dc f", dc=DC),
                 w1_d[e + 2].rearrange("(dc p) f -> p dc f", p=128), inc=[(gw[buf], 16)])
        r.wait("tensor", V, v_A_done[(5, 1)])
        eo_idx = 0
        v_eo_evac = {}
        for dc in range(DC):
            zb = (dc % 2) * 2048
            if dc >= 2:
                r.wait("vector", ga[dc % 2], dg_acc[(e, dc - 2)])
            elif e > 0:
                r.wait("vector", ga[dc % 2], dg_acc[(e - 1, dc + 6)])
            for half in range(2):
                P1 = PA1 if (eo_idx % 2 == 0) else PB1
                if eo_idx >= 2:
                    r.wait("tensor", V, v_eo_evac[eo_idx - 2])
                for hc in range(6):
                    for st in range(2):
                        r.op("tensor", "matmul", P1[:, st * 512:(st + 1) * 512],
                             w2sb[:, buf * 6144 + hc * 1024 + dc * 128:
                                  buf * 6144 + hc * 1024 + dc * 128 + 128],
                             Abuf[:, hc * 2048 + half * 1024 + st * 512:
                                  hc * 2048 + half * 1024 + st * 512 + 512],
                             start=(hc == 0), stop=(hc == 5))
                r.inc_last("tensor", T)
                t_eo = T.v
                r.wait("vector", T, t_eo)
                r.op("vector", "tensor_copy",
                     zst3[:, zb + half * 1024: zb + half * 1024 + 1024], P1[:, :],
                     inc=[(V, 1)])
                v_eo_evac[eo_idx] = V.v
                eo_idx += 1
            v_z3evac[(e, dc)] = V.v
            r.wait("gpsimd", V, V.v)
            r.op("gpsimd", "dma_start", z_d[dc, :, :], zst3[:, zb:zb + 2048],
                 accum_op=ALU.add, inc=[(ga[dc % 2], 16)])
            dg_acc[(e, dc)] = ga[dc % 2].v
        t_eo_done[e] = T.v
        v_last_eoevac = V.v
        if e + 2 < NE:
            r.wait("gpsimd", T, t_eo_done[e])
            r.op("gpsimd", "dma_start",
                 w2sb[:, buf * 6144:(buf + 1) * 6144].rearrange("p (hc c) -> p hc c", hc=6),
                 w2_d[e + 2].rearrange("(hc p) c -> p hc c", p=128), inc=[(gw[buf], 16)])
            dg_w[e + 2] = gw[buf].v
        v_A_done = {(5, 0): v_A_done[(5, 0)], (5, 1): v_A_done[(5, 1)]}
        v_A_done = {(-1, 0): v_A_done[(5, 0)], (-1, 1): v_A_done[(5, 1)]}
    ga_fin = (ga[0].v, ga[1].v)

    if phases <= 3:
        _dbg_z(r, nc, out_d, z_d, sdbg, (T, V, S), all_dma)
    r.emit(nc)
    c_sc.close()
    if phases <= 3:
        top.close(); _lp.__exit__(None, None, None)
        return nc

    d_sc = ExitStack()
    zin = d_sc.enter_context(nc.sbuf_tensor("zin", [128, 2048], F32))
    xtb = d_sc.enter_context(nc.sbuf_tensor("xtb", [128, 2048], F32))
    zhi = d_sc.enter_context(nc.sbuf_tensor("zhi", [128, 2048], BF16))
    zhi32 = d_sc.enter_context(nc.sbuf_tensor("zhi32", [128, 2048], F32))
    zlo = d_sc.enter_context(nc.sbuf_tensor("zlo", [128, 2048], BF16))
    osb = d_sc.enter_context(nc.sbuf_tensor("osb", [128, 2048], F32))
    PO1 = d_sc.enter_context(nc.psum_tensor("PO1", [128, 1024], F32))
    PO2 = d_sc.enter_context(nc.psum_tensor("PO2", [128, 1024], F32))

    r = Rec(SELFS)
    r.auto_chain("vector", True)
    r.wait("sync", ga[0], ga_fin[0])
    r.wait("sync", ga[1], ga_fin[1])
    z_r = z_d.rearrange("dc p t -> p dc t")
    v_used = {}
    t_tr4 = {}
    ds_out = {}
    for tb in range(16):
        bb = (tb % 2) * 1024
        s4p = s4[tb % 2]
        if tb >= 2:
            r.wait("sync", V, v_used[tb - 2])
        r.op("sync", "dma_start", zin[:, bb:bb + 1024].rearrange("p (dc c) -> p dc c", dc=DC),
             z_r[:, :, tb * 128:(tb + 1) * 128], inc=[(s4p, 16)])
        r.op("sync", "dma_start", xtb[:, bb:bb + 1024], xo_r[:, tb, :], inc=[(s4p, 16)])
        ds_in = s4p.v
        r.wait("vector", s4p, ds_in)
        if tb >= 2:
            r.wait("vector", T, t_tr4[tb - 2])
            r.wait("vector", so[tb % 2], ds_out[tb - 2])
        r.op("vector", "tensor_copy", zhi[:, bb:bb + 1024], zin[:, bb:bb + 1024])
        r.op("vector", "tensor_copy", zhi32[:, bb:bb + 1024], zhi[:, bb:bb + 1024])
        r.op("vector", "tensor_sub", zlo[:, bb:bb + 1024], zin[:, bb:bb + 1024],
             zhi32[:, bb:bb + 1024], inc=[(V, 1)])
        v_hl = V.v
        POx = PO1 if tb % 2 == 0 else PO2
        r.wait("tensor", V, v_hl)
        for dc in range(DC):
            for srci, srcm in ((0, zhi), (1, zlo)):
                r.op("tensor", "matmul", POx[:, dc * 128:(dc + 1) * 128],
                     srcm[:, bb + dc * 128: bb + (dc + 1) * 128], id_sb[:, :],
                     start=(dc % 4 == 0 and srci == 0), stop=(dc % 4 == 3 and srci == 1),
                     skip_group_check=True)
        r.inc_last("tensor", T)
        t_tr4[tb] = T.v
        r.wait("vector", T, T.v)
        r.op("vector", "tensor_add", osb[:, bb:bb + 1024], xtb[:, bb:bb + 1024], POx[:, :],
             inc=[(V, 1)])
        v_used[tb] = V.v
        r.wait("sync", V, V.v)
        r.op("sync", "dma_start", out_d[tb * 128:(tb + 1) * 128, :], osb[:, bb:bb + 1024],
             inc=[(so[tb % 2], 16)])
        ds_out[tb] = so[tb % 2].v
    for ctr in all_dma:
        r.wait("sync", ctr, ctr.v)
    r.emit(nc)
    d_sc.close()
    top.close()
    _lp.__exit__(None, None, None)
    return nc


def _dbg_z(r, nc, out_d, z_d, sdbg, ctrs, all_dma):
    for ctr in ctrs:
        r.wait("gpsimd", ctr, ctr.v)
    for ctr in all_dma:
        r.wait("gpsimd", ctr, ctr.v)
    r.op("gpsimd", "dma_start", out_d.rearrange("(dc p) t -> dc p t", p=128),
         z_d, inc=[(sdbg, 16)])
    r.wait("gpsimd", sdbg, sdbg.v)


def _dbg_out(r, nc, out_d, xn, sdbg, ctrs, all_dma):
    for ctr in ctrs:
        r.wait("gpsimd", ctr, ctr.v)
    r.op("gpsimd", "dma_start", out_d.rearrange("(a p) c -> p a c", p=128),
         xn[:, :].rearrange("p (a c) -> p a c", a=16), inc=[(sdbg, 16)])
    for ctr in all_dma:
        r.wait("gpsimd", ctr, ctr.v)


_NC = None


def _host_tables(qs):
    pos = np.arange(L)
    invf = (1.0 / ROPE_BASE ** (np.arange(0, HD, 2) / HD)).astype(np.float64)
    ang = pos[:, None].astype(np.float64) * invf[None, :]
    cos_t = np.cos(ang).astype(np.float32)
    sin_t = np.sin(ang).astype(np.float32)
    cos64 = np.concatenate([cos_t.T, cos_t.T], axis=0)
    sin64 = np.concatenate([-sin_t.T, sin_t.T], axis=0)
    cosk = np.concatenate([cos64, cos64], axis=0)
    sink = np.concatenate([sin64, sin64], axis=0)
    cosq = cosk[:, qs * OWN:(qs + 1) * OWN] * 0.125
    sinq = sink[:, qs * OWN:(qs + 1) * OWN] * 0.125
    slopes = np.asarray(_alibi_slopes(H), np.float32)
    alibi = np.zeros((128, 1024), np.float32)
    for h in range(H):
        for kc in range(64):
            j = pos[kc * 128:(kc + 1) * 128].astype(np.float32)
            alibi[:, h * 64 + kc] = slopes[h] * (j - (L - 1))
    return (cosk.astype(ml_dtypes.bfloat16), sink.astype(ml_dtypes.bfloat16),
            cosq.astype(ml_dtypes.bfloat16), sinq.astype(ml_dtypes.bfloat16), alibi)


def prep_in_maps(L0_attn_x, L0_attn_Wq, L0_attn_Wk, L0_attn_Wv, L0_attn_Wo,
                 L0_ffn_gate_w, L0_ffn_W1, L0_ffn_W2, L0_ffn_Ws1, L0_ffn_Ws2):
    x = np.ascontiguousarray(np.asarray(L0_attn_x, np.float32))
    Wq = np.asarray(L0_attn_Wq, np.float32)
    Wk = np.ascontiguousarray(np.asarray(L0_attn_Wk, np.float32))
    Wv = np.ascontiguousarray(np.asarray(L0_attn_Wv, np.float32))
    Wo = np.ascontiguousarray(np.asarray(L0_attn_Wo, np.float32))
    gw = np.asarray(L0_ffn_gate_w, np.float32)
    W1 = np.asarray(L0_ffn_W1, np.float32)
    W2 = np.asarray(L0_ffn_W2, np.float32)
    Ws1 = np.asarray(L0_ffn_Ws1, np.float32)
    Ws2 = np.asarray(L0_ffn_Ws2, np.float32)

    colperm = np.concatenate([np.arange(h * 64, (h + 1) * 64)
                              for pr in PAIRS for h in pr])
    wqsh = np.ascontiguousarray(Wq[:, colperm])
    gt = np.ascontiguousarray(gw.T)
    w1t = np.empty((NE, D, 1536), np.float32)
    w2t = np.empty((NE, 768, D), np.float32)
    for e in range(8):
        w1t[e] = W1[e].T
        w2t[e] = W2[e].T
    for sh in range(2):
        w1t[8 + sh] = np.concatenate([Ws1[sh * 768:(sh + 1) * 768],
                                      Ws1[1536 + sh * 768: 1536 + (sh + 1) * 768]],
                                     axis=0).T
        w2t[8 + sh] = Ws2.T[sh * 768:(sh + 1) * 768]
    w1t = np.ascontiguousarray(w1t)
    w2t = np.ascontiguousarray(w2t)
    ident = np.eye(128, dtype=np.float32).astype(ml_dtypes.bfloat16)
    sel = np.zeros((8, 1024), np.float32)
    for e in range(8):
        sel[e, e * 128:(e + 1) * 128] = 1.0
    sel = sel.astype(ml_dtypes.bfloat16)

    in_maps = []
    for c in range(8):
        b, qs = c // 4, c % 4
        cosk, sink, cosq, sinq, alibi = _host_tables(qs)
        xb = np.ascontiguousarray(x[b])
        xown = np.ascontiguousarray(x[b, qs * OWN:(qs + 1) * OWN])
        in_maps.append({
            "xb": xb, "xown": xown, "wqsh": wqsh, "wk": Wk, "wv": Wv, "wo": Wo, "gt": gt,
            "w1t": w1t, "w2t": w2t, "cosk": cosk, "sink": sink, "cosq": cosq,
            "sinq": sinq, "alibi": alibi, "ident": ident, "sel": sel,
        })
    return in_maps


_NCS = {}


def kernel(**inputs):
    import os
    phases = int(os.environ.get("KMK_PHASES", "4"))
    nonce0 = int(os.environ.get("KMK_NONCE", "0"))
    in_maps = prep_in_maps(**inputs)
    ncores = int(os.environ.get("KMK_CORES", "8"))
    in_maps = in_maps[:ncores]
    res = None
    last = None
    for k in range(nonce0, nonce0 + 6):
        key = (phases, k)
        if key not in _NCS:
            _NCS[key] = build_nc(phases=phases, nonce=k,
                                 stripes=int(os.environ.get("KMK_STRIPES", "16")))
        try:
            res = run_bass_kernel_spmd(_NCS[key], in_maps, core_ids=list(range(ncores)))
            break
        except Exception as e:
            last = e
            print(f"[kernel] launch attempt nonce={k} failed: {type(e).__name__}; retrying", flush=True)
    if res is None:
        raise last
    out = np.zeros((B, L, D), np.float32)
    for c in range(ncores):
        b, qs = c // 4, c % 4
        out[b, qs * OWN:(qs + 1) * OWN, :] = res.results[c]["out"]
    return out


# version 13
# speedup vs baseline: 1.0522x; 1.0522x over previous
import numpy as np
import ml_dtypes
from contextlib import ExitStack

import concourse.bass as bass
import concourse.mybir as mybir
from concourse.bass_utils import run_bass_kernel_spmd

F32 = mybir.dt.float32
BF16 = mybir.dt.bfloat16
AF = mybir.ActivationFunctionType
ALU = mybir.AluOpType
AX = mybir.AxisListType.X

B, L, D = 2, 8192, 1024
H, KV, HD = 16, 4, 64
OWN = 2048
DC = 8
NE = 10
PAIRS = [(0, 4), (1, 5), (2, 6), (3, 7), (8, 12), (9, 13), (10, 14), (11, 15)]
ROPE_BASE = 10000.0
ALIBI_SKIP_T = 170.0


def _keep_chunks():
    slopes = _alibi_slopes(H)
    keep = []
    for h in range(H):
        ks = [kc for kc in range(64)
              if slopes[h] * ((L - 1) - (kc * 128 + 127)) <= ALIBI_SKIP_T]
        keep.append(sorted(ks))
    return keep


def _alibi_slopes(n_heads):
    import math
    start = 2.0 ** (-2.0 ** (-(math.log2(n_heads) - 3)))
    return [start * start ** i for i in range(n_heads)]


class Ctr:
    def __init__(self, es, nc, name):
        self.sem = es.enter_context(nc.semaphore(name))
        self.v = 0


class Rec:
    ENGS = ("tensor", "vector", "scalar", "sync", "gpsimd")

    def __init__(self, selfsems=None):
        self.q = {e: [] for e in self.ENGS}
        self.last = {}
        self.lastinfo = {}
        self.selfsems = selfsems or {}
        self.autoc = {}

    def op(self, eng, method, *a, inc=None, **kw):
        incs = []
        if inc:
            for ctr, k in inc:
                ctr.v += k
                incs.append((ctr.sem, k))
        if self.autoc.get(eng):
            self.chain(eng)
        cell = {}
        def f(e, m=method, a=a, kw=kw, incs=tuple(incs), cell=cell):
            ins = getattr(e, m)(*a, **kw)
            cell["i"] = ins
            for s, k in incs:
                ins.then_inc(s, k)
        self.q[eng].append(f)
        self.last[eng] = cell
        self.lastinfo[eng] = (inc[0][0], inc[0][0].v) if inc else None

    def inc_last(self, eng, ctr, k=1):
        ctr.v += k
        cell = self.last[eng]
        self.q[eng].append(lambda e, c=cell, s=ctr.sem, kk=k: c["i"].then_inc(s, kk))
        self.lastinfo[eng] = (ctr, ctr.v)

    def chain(self, eng):
        info = self.lastinfo.get(eng)
        if info is None:
            if eng not in self.last:
                return
            ctr = self.selfsems[eng]
            self.inc_last(eng, ctr)
            info = (ctr, ctr.v)
        ctr, v = info
        self.q[eng].append(lambda e, s=ctr.sem, n=v: e.wait_ge(s, n))
        self.lastinfo[eng] = None

    def auto_chain(self, eng, on):
        self.autoc[eng] = on

    def wait(self, eng, ctr, v=None):
        vv = ctr.v if v is None else v
        if vv <= 0:
            return
        self.q[eng].append(lambda e, s=ctr.sem, n=vv: e.wait_ge(s, n))

    def raw(self, eng, fn):
        self.q[eng].append(fn)

    def emit(self, nc):
        with nc.Block() as blk:
            for name in self.ENGS:
                ops = self.q[name]
                if not ops:
                    continue
                def body(e, ops=ops):
                    for f in ops:
                        f(e)
                getattr(blk, name)(body)


def _act_set_id(nc, set_name):
    from concourse.hw_specs import get_activation_tables
    names = list(get_activation_tables(nc.m.arch).keys())
    return names.index(set_name)


def _load_table(rec, set_id):
    def f(e, idx=set_id):
        e.add_instruction(mybir.InstLoadActFuncSet(
            name=e.bass.get_next_instruction_name(),
            act_func_set_id=idx, ins=[], outs=[]))
    rec.raw("scalar", f)


def build_nc(phases=4, nonce=0, stripes=16):
    import os as _os
    CUT = set(_os.environ.get("KMK_CUT", "").split(","))
    nc = bass.Bass("TRN2", target_bir_lowering=False)
    _lp = nc.allow_low_precision("bf16 elementwise outputs feed bf16 matmul operands by design")
    _lp.__enter__()

    xp_d = nc.dram_tensor("xb", [L, D], F32, kind="ExternalInput").ap()
    xo_d = nc.dram_tensor("xown", [OWN, D], F32, kind="ExternalInput").ap()
    wq_d = nc.dram_tensor("wqsh", [D, D], F32, kind="ExternalInput").ap()
    wk_d = nc.dram_tensor("wk", [D, 256], F32, kind="ExternalInput").ap()
    wv_d = nc.dram_tensor("wv", [D, 256], F32, kind="ExternalInput").ap()
    wo_d = nc.dram_tensor("wo", [D, D], F32, kind="ExternalInput").ap()
    gt_d = nc.dram_tensor("gt", [D, 8], F32, kind="ExternalInput").ap()
    w1_d = nc.dram_tensor("w1t", [NE, D, 1536], F32, kind="ExternalInput").ap()
    w2_d = nc.dram_tensor("w2t", [NE, 768, D], F32, kind="ExternalInput").ap()
    cosk_d = nc.dram_tensor("cosk", [128, L], BF16, kind="ExternalInput").ap()
    sink_d = nc.dram_tensor("sink", [128, L], BF16, kind="ExternalInput").ap()
    cosq_d = nc.dram_tensor("cosq", [128, OWN], BF16, kind="ExternalInput").ap()
    sinq_d = nc.dram_tensor("sinq", [128, OWN], BF16, kind="ExternalInput").ap()
    alibi_d = nc.dram_tensor("alibi", [128, 1024], F32, kind="ExternalInput").ap()
    id_d = nc.dram_tensor("ident", [128, 128], BF16, kind="ExternalInput").ap()
    sel_d = nc.dram_tensor("sel", [8, 1024], BF16, kind="ExternalInput").ap()
    out_d = nc.dram_tensor("out", [OWN, D], F32, kind="ExternalOutput").ap()
    z_d = nc.dram_tensor("zacc", [DC, 128, OWN], F32).ap()

    KEEP = _keep_chunks()
    NLE = _act_set_id(nc, "natural_log_exp_and_others")
    SIL = _act_set_id(nc, "silu_and_others")

    top = ExitStack()
    xn = top.enter_context(nc.sbuf_tensor("xn", [128, DC * OWN], BF16))
    wfm = top.enter_context(nc.sbuf_tensor("wfm", [8, OWN], F32))
    wfmb = top.enter_context(nc.sbuf_tensor("wfmb", [8, OWN], BF16))
    ones64 = top.enter_context(nc.sbuf_tensor("ones64", [65, 128], BF16))
    id_sb = top.enter_context(nc.sbuf_tensor("id_sb", [128, 128], BF16))
    alibi_sb = top.enter_context(nc.sbuf_tensor("alibi_sb", [128, 1024], F32))
    cqd = top.enter_context(nc.sbuf_tensor("cqd", [128, OWN], BF16))
    sqd = top.enter_context(nc.sbuf_tensor("sqd", [128, OWN], BF16))


    ab = ExitStack()
    K01 = ab.enter_context(nc.sbuf_tensor("K01", [128, L], BF16))
    K23 = ab.enter_context(nc.sbuf_tensor("K23", [128, L], BF16))
    Vt = [ab.enter_context(nc.sbuf_tensor(f"V{k}", [128, 64 * 65], BF16)) for k in range(KV)]

    sset = Ctr(top, nc, "sset")
    gset = Ctr(top, nc, "gset")
    sx = [Ctr(top, nc, "sx0"), Ctr(top, nc, "sx1")]
    scs = Ctr(top, nc, "scs")
    sz = [Ctr(top, nc, "sz0"), Ctr(top, nc, "sz1")]
    gw = [Ctr(top, nc, "gw0"), Ctr(top, nc, "gw1")]
    ga = [Ctr(top, nc, "ga0"), Ctr(top, nc, "ga1")]
    s4 = [Ctr(top, nc, "s4a"), Ctr(top, nc, "s4b")]
    so = [Ctr(top, nc, "so0"), Ctr(top, nc, "so1")]
    sdbg = Ctr(top, nc, "sdbg")
    all_dma = [sset, gset, scs, sdbg] + sx + sz + gw + ga + s4 + so
    T = Ctr(top, nc, "T")
    V = Ctr(top, nc, "Vv")
    S = Ctr(top, nc, "S")
    VSF = Ctr(top, nc, "VSF")
    SSF = Ctr(top, nc, "SSF")
    SELFS = {"vector": VSF, "scalar": SSF}

    a_sc = ExitStack()
    wk_sb = a_sc.enter_context(nc.sbuf_tensor("wk_sb", [128, 2048], BF16))
    wv_sb = a_sc.enter_context(nc.sbuf_tensor("wv_sb", [128, 2048], BF16))
    gt32 = a_sc.enter_context(nc.sbuf_tensor("gt32", [128, 64], F32))
    ghi = a_sc.enter_context(nc.sbuf_tensor("ghi", [128, 64], BF16))
    ghi32 = a_sc.enter_context(nc.sbuf_tensor("ghi32", [128, 64], F32))
    glo = a_sc.enter_context(nc.sbuf_tensor("glo", [128, 64], BF16))
    coskS = a_sc.enter_context(nc.sbuf_tensor("coskS", [128, 512], BF16))
    sinkS = a_sc.enter_context(nc.sbuf_tensor("sinkS", [128, 512], BF16))
    xt = a_sc.enter_context(nc.sbuf_tensor("xt", [128, 2048], F32))
    xn32 = a_sc.enter_context(nc.sbuf_tensor("xn32", [128, 1024], F32))
    xnhi32 = a_sc.enter_context(nc.sbuf_tensor("xnhi32", [128, 1024], F32))
    xntm = a_sc.enter_context(nc.sbuf_tensor("xntm", [128, 2048], BF16))
    lotm = a_sc.enter_context(nc.sbuf_tensor("lotm", [128, 2048], BF16))
    fmstripe = a_sc.enter_context(nc.sbuf_tensor("fmstripe", [128, 4096], BF16))
    kraw = a_sc.enter_context(nc.sbuf_tensor("kraw", [128, 512], BF16))
    kshuf = a_sc.enter_context(nc.sbuf_tensor("kshuf", [128, 512], BF16))
    rt1 = a_sc.enter_context(nc.sbuf_tensor("rt1", [128, 512], BF16))
    rt2 = a_sc.enter_context(nc.sbuf_tensor("rt2", [128, 512], BF16))
    vstage = a_sc.enter_context(nc.sbuf_tensor("vstage", [128, 512], BF16))
    ssum = a_sc.enter_context(nc.sbuf_tensor("ssum", [128, 64], F32))
    qsum = a_sc.enter_context(nc.sbuf_tensor("qsum", [128, 64], F32))
    e2s = a_sc.enter_context(nc.sbuf_tensor("e2s", [128, 64], F32))
    mus = a_sc.enter_context(nc.sbuf_tensor("mus", [128, 64], F32))
    vvs = a_sc.enter_context(nc.sbuf_tensor("vvs", [128, 64], F32))
    lns = a_sc.enter_context(nc.sbuf_tensor("lns", [128, 64], F32))
    rstds = a_sc.enter_context(nc.sbuf_tensor("rstds", [128, 64], F32))
    nbs = a_sc.enter_context(nc.sbuf_tensor("nbs", [128, 64], F32))
    sqscr = a_sc.enter_context(nc.sbuf_tensor("sqscr", [128, 1024], BF16))
    logits = a_sc.enter_context(nc.sbuf_tensor("logits", [8, 2048], F32))
    id32 = a_sc.enter_context(nc.sbuf_tensor("id32", [128, 128], F32))
    eps_sb = a_sc.enter_context(nc.sbuf_tensor("eps_sb", [128, 1], F32))
    ltm = a_sc.enter_context(nc.sbuf_tensor("ltm", [128, 128], F32))
    eL = a_sc.enter_context(nc.sbuf_tensor("eL", [128, 128], F32))
    probs = a_sc.enter_context(nc.sbuf_tensor("probs", [128, 128], F32))
    maskd = a_sc.enter_context(nc.sbuf_tensor("maskd", [128, 128], F32))
    scr8 = a_sc.enter_context(nc.sbuf_tensor("scr8", [128, 8], F32))
    m1p = a_sc.enter_context(nc.sbuf_tensor("m1p", [128, 16], F32))
    m2p = a_sc.enter_context(nc.sbuf_tensor("m2p", [128, 16], F32))
    rsum = a_sc.enter_context(nc.sbuf_tensor("rsum", [128, 16], F32))
    rrec = a_sc.enter_context(nc.sbuf_tensor("rrec", [128, 16], F32))
    TP = a_sc.enter_context(nc.psum_tensor("TP", [128, 1024], F32))
    TP2 = a_sc.enter_context(nc.psum_tensor("TP2", [128, 1024], F32))
    KP = a_sc.enter_context(nc.psum_tensor("KP", [128, 512], F32))
    VP = a_sc.enter_context(nc.psum_tensor("VP", [128, 512], F32))
    VT = a_sc.enter_context(nc.psum_tensor("VT", [128, 512], F32))
    GP = a_sc.enter_context(nc.psum_tensor("GP", [8, 512], F32))

    r = Rec(SELFS)
    r.auto_chain("vector", True)
    r.auto_chain("scalar", True)
    _load_table(r, NLE)
    for _ in range(nonce):
        r.op("tensor", "nop")

    r.op("sync", "dma_start", alibi_sb[:, :], alibi_d, inc=[(sset, 16)])
    r.op("sync", "dma_start", id_sb[:, :], id_d, inc=[(sset, 16)])
    r.op("sync", "dma_start", cqd[:, :], cosq_d, inc=[(sset, 16)])
    r.op("sync", "dma_start", sqd[:, :], sinq_d, inc=[(sset, 16)])
    r.op("sync", "dma_start", gt32[:, :].rearrange("p (dc e) -> p dc e", dc=8),
         gt_d.rearrange("(dc p) e -> p dc e", p=128), inc=[(sset, 16)])
    ds_setup = sset.v
    r.op("gpsimd", "dma_start", wk_sb[:, :].rearrange("p (dc c) -> p dc c", dc=8),
         wk_d.rearrange("(dc p) c -> p dc c", p=128), inc=[(gset, 16)])
    r.op("gpsimd", "dma_start", wv_sb[:, :].rearrange("p (dc c) -> p dc c", dc=8),
         wv_d.rearrange("(dc p) c -> p dc c", p=128), inc=[(gset, 16)])
    dg_w0 = gset.v

    r.op("vector", "memset", ones64[64:65, :], 1.0)
    for k in range(KV):
        r.op("vector", "memset",
             Vt[k][:, :].rearrange("p (kc c) -> p kc c", c=65)[:, :, 64:65], 1.0)
    r.wait("vector", sset, ds_setup)
    r.op("vector", "tensor_copy", id32[:, :], id_sb[:, :])
    r.op("vector", "memset", eps_sb[:, :], 1e-5)
    r.op("vector", "tensor_copy", ghi[:, :], gt32[:, :])
    r.op("vector", "tensor_copy", ghi32[:, :], ghi[:, :])
    r.op("vector", "tensor_sub", glo[:, :], gt32[:, :], ghi32[:, :], inc=[(V, 1)])
    v_ginit = V.v

    xp_r = xp_d.rearrange("(t p) c -> p t c", p=128)
    xo_r = xo_d.rearrange("(t p) c -> p t c", p=128)
    v_xt_done = {}
    s_xt_done = {}
    t_xntm_done = {}
    t_lotm_done = {}
    v_cos_done = {}
    t_kv_done = 0
    t_gate_prev = 0
    v_logit_evac = 0
    v_kevac = 0

    for s in range(stripes + 4):
        own = s >= stripes
        sow = s - stripes
        src_r = xo_r if own else xp_r
        fm_guard = max(t_kv_done, t_gate_prev)
        cb = 0
        for i in range(4):
            tau = 4 * s + i
            lt = (4 * sow + i) if own else tau
            tb = (tau % 2) * 1024
            if tau >= 2:
                r.wait("sync", V, v_xt_done[tau - 2])
                r.wait("sync", S, s_xt_done[tau - 2])
            sxp = sx[tau % 2]
            r.op("sync", "dma_start", xt[:, tb:tb + 1024], src_r[:, lt, :], inc=[(sxp, 16)])
            ds_tile = sxp.v

            r.wait("vector", sxp, ds_tile)
            r.op("vector", "reduce_sum", ssum[:, tau % 64:tau % 64 + 1], xt[:, tb:tb + 1024], axis=AX)
            r.wait("scalar", sxp, ds_tile)
            r.op("scalar", "activation", sqscr[:, :], xt[:, tb:tb + 1024], AF.Square,
                 accum_out=qsum[:, tau % 64:tau % 64 + 1], inc=[(S, 1)])
            s_sq = S.v
            s_xt_done[tau] = s_sq
            c = tau % 64
            r.wait("vector", S, s_sq)
            r.op("vector", "tensor_single_scalar", mus[:, c:c + 1], ssum[:, c:c + 1],
                 1.0 / 1024.0, ALU.mult)
            r.op("vector", "tensor_scalar", vvs[:, c:c + 1], mus[:, c:c + 1],
                 mus[:, c:c + 1], None, ALU.mult)
            r.op("vector", "scalar_tensor_tensor", vvs[:, c:c + 1], qsum[:, c:c + 1],
                 -1.0 / 1024.0, vvs[:, c:c + 1], ALU.mult, ALU.add, inc=[(V, 1)])
            v_vv = V.v
            r.wait("scalar", V, v_vv)
            r.op("scalar", "activation", lns[:, c:c + 1], vvs[:, c:c + 1], AF.Ln,
                 bias=eps_sb[:, 0:1], scale=-1.0)
            r.op("scalar", "activation", rstds[:, c:c + 1], lns[:, c:c + 1], AF.Exp,
                 scale=-0.5, inc=[(S, 1)])
            s_rstd = S.v
            r.wait("vector", S, s_rstd)
            if tau >= 2:
                r.wait("vector", T, t_xntm_done[tau - 2])
            if own:
                r.op("vector", "tensor_scalar", xn32[:, :], xt[:, tb:tb + 1024],
                     mus[:, c:c + 1], rstds[:, c:c + 1], ALU.subtract, ALU.mult)
                r.op("vector", "tensor_copy", xntm[:, tb:tb + 1024], xn32[:, :])
                if tau >= 2 and (tau - 2) in t_lotm_done:
                    r.wait("vector", T, t_lotm_done[tau - 2])
                r.op("vector", "tensor_copy", xnhi32[:, :], xntm[:, tb:tb + 1024])
                r.op("vector", "tensor_sub", lotm[:, tb:tb + 1024], xn32[:, :],
                     xnhi32[:, :], inc=[(V, 1)])
            else:
                r.op("vector", "tensor_scalar", xntm[:, tb:tb + 1024], xt[:, tb:tb + 1024],
                     mus[:, c:c + 1], rstds[:, c:c + 1], ALU.subtract, ALU.mult,
                     inc=[(V, 1)])
            v_xt_done[tau] = V.v
            v_aff = V.v

            r.wait("tensor", V, v_aff)
            TPb = TP if (tau % 2) == 0 else TP2
            for dc in range(DC):
                r.op("tensor", "matmul", TPb[:, dc * 128:dc * 128 + 128],
                     xntm[:, tb + dc * 128:tb + dc * 128 + 128], id_sb[:, :],
                     start=(dc % 4 == 0), stop=(dc % 4 == 3), skip_group_check=True)
            r.inc_last("tensor", T)
            t_hi = T.v
            r.wait("vector", T, t_hi)
            if i == 0:
                r.wait("vector", T, fm_guard)
            if own:
                dst = xn[:, :].rearrange("p (dc t) -> p dc t", dc=DC)[:, :, (4 * sow + i) * 128:(4 * sow + i) * 128 + 128]
            else:
                dst = fmstripe[:, :].rearrange("p (dc t) -> p dc t", dc=DC)[:, :, i * 128:i * 128 + 128]
            r.op("vector", "tensor_copy", dst,
                 TPb[:, :].rearrange("p (dc t) -> p dc t", dc=DC), inc=[(V, 1)])
            v_hi_evac = V.v
            if own:
                r.wait("tensor", V, v_hi_evac)
                for dc in range(DC):
                    r.op("tensor", "matmul", TPb[:, dc * 128:dc * 128 + 128],
                         lotm[:, tb + dc * 128:tb + dc * 128 + 128], id_sb[:, :],
                         start=(dc % 4 == 0), stop=(dc % 4 == 3), skip_group_check=True)
                r.inc_last("tensor", T)
                t_lotm_done[tau] = T.v
                t_xntm_done[tau] = T.v
                r.wait("vector", T, T.v)
                r.op("vector", "tensor_copy",
                     fmstripe[:, :].rearrange("p (dc t) -> p dc t", dc=DC)[:, :, i * 128:i * 128 + 128],
                     TPb[:, :].rearrange("p (dc t) -> p dc t", dc=DC), inc=[(V, 1)])
            else:
                t_xntm_done[tau] = t_hi
        v_stripe = V.v
        if own:
            pass
        else:
            if s >= 1:
                r.wait("sync", V, v_cos_done[s - 1])
            r.op("sync", "dma_start", coskS[:, cb:cb + 512], cosk_d[:, s * 512:(s + 1) * 512],
                 inc=[(scs, 16)])
            r.op("sync", "dma_start", sinkS[:, cb:cb + 512], sink_d[:, s * 512:(s + 1) * 512],
                 inc=[(scs, 16)])
            ds_cos = scs.v

        def xn_src(dc, st=s):
            return fmstripe[:, dc * 512: dc * 512 + 512]

        if "kv" in CUT or own:
            if own and "gate" not in CUT:
                r.wait("tensor", V, v_ginit)
                r.wait("tensor", V, v_stripe)
                if v_logit_evac:
                    r.wait("tensor", V, v_logit_evac)
                first = True
                for gsrc, gw_ in ((0, ghi), (0, glo), (1, ghi)):
                    for dc in range(DC):
                        if gsrc == 0:
                            rhs = xn[:, dc * OWN + sow * 512: dc * OWN + sow * 512 + 512]
                        else:
                            rhs = fmstripe[:, dc * 512: dc * 512 + 512]
                        r.op("tensor", "matmul", GP[:, :], gw_[:, dc * 8:dc * 8 + 8], rhs,
                             start=first, stop=(gsrc == 1 and dc == DC - 1))
                        first = False
                r.inc_last("tensor", T)
                t_gate_prev = T.v
                r.wait("vector", T, t_gate_prev)
                r.op("vector", "tensor_copy", logits[:, sow * 512:(sow + 1) * 512], GP[:, :],
                     inc=[(V, 1)])
                v_logit_evac = V.v
            t_kv_done = t_gate_prev if own else t_kv_done
            continue
        if s == 0:
            r.wait("tensor", gset, dg_w0)
        r.wait("tensor", V, v_stripe)
        for cch in range(1 if "cch1" in CUT else 2):
            Kdst = K01 if cch == 0 else K23
            if v_kevac:
                r.wait("tensor", V, v_kevac)
            for dc in range(DC):
                r.op("tensor", "matmul", KP[:, :],
                     wk_sb[:, dc * 256 + cch * 128: dc * 256 + cch * 128 + 128],
                     xn_src(dc), start=(dc == 0), stop=(dc == DC - 1))
            r.inc_last("tensor", T)
            t_k = T.v
            r.wait("vector", T, t_k)
            r.op("vector", "tensor_copy", kraw[:, :], KP[:, :], inc=[(V, 1)])
            if "rope" in CUT:
                r.op("vector", "tensor_copy", Kdst[:, s * 512:(s + 1) * 512], kraw[:, :])
                skiprope = True
            else:
                skiprope = False
            if not skiprope:
                r.op("vector", "tensor_copy", kshuf[0:32, :], kraw[32:64, :])
                r.op("vector", "tensor_copy", kshuf[32:64, :], kraw[0:32, :])
                r.op("vector", "tensor_copy", kshuf[64:96, :], kraw[96:128, :])
                r.op("vector", "tensor_copy", kshuf[96:128, :], kraw[64:96, :])
                r.wait("vector", scs, ds_cos)
                r.op("vector", "tensor_mul", rt1[:, :], kraw[:, :], coskS[:, cb:cb + 512])
                r.op("vector", "tensor_mul", rt2[:, :], kshuf[:, :], sinkS[:, cb:cb + 512])
                r.op("vector", "tensor_add", Kdst[:, s * 512:(s + 1) * 512], rt1[:, :], rt2[:, :])
            for dc in range(DC):
                r.op("tensor", "matmul", VP[:, :],
                     wv_sb[:, dc * 256 + cch * 128: dc * 256 + cch * 128 + 128],
                     xn_src(dc), start=(dc == 0), stop=(dc == DC - 1))
            r.inc_last("tensor", T)
            t_v = T.v
            r.wait("vector", T, t_v)
            r.op("vector", "tensor_copy", vstage[:, :], VP[:, :], inc=[(V, 1)])
            v_vstage = V.v
            if "vtr" in CUT:
                v_kevac = V.v
                continue
            r.wait("tensor", V, v_vstage)
            for kvl in range(2):
                if kvl == 1:
                    r.wait("tensor", V, V.v)
                for tb4 in range(4):
                    r.op("tensor", "matmul", VT[:, tb4 * 64:tb4 * 64 + 64],
                         vstage[kvl * 64:(kvl + 1) * 64, tb4 * 128:(tb4 + 1) * 128],
                         id_sb[kvl * 64:(kvl + 1) * 64, kvl * 64:kvl * 64 + 64],
                         start=(tb4 == 0), stop=(tb4 == 3), skip_group_check=True)
                r.inc_last("tensor", T)
                r.wait("vector", T, T.v)
                kvi = cch * 2 + kvl
                src = VT[:, 0:256].rearrange("p (t c) -> p t c", t=4)
                dst = Vt[kvi][:, :].rearrange("p (kc c) -> p kc c", c=65)[:, s * 4:s * 4 + 4, 0:64]
                r.op("vector", "tensor_copy", dst, src, inc=[(V, 1)])
            v_kevac = V.v
        v_cos_done[s] = V.v
        t_kv_done = T.v


    if stripes < 4:
        _dbg_out(r, nc, out_d, xn, sdbg, (T, V, S), all_dma)
        r.emit(nc)
        a_sc.close()
        ab.close(); top.close(); _lp.__exit__(None, None, None)
        return nc
    r.wait("tensor", V, V.v)
    for tb in range(16):
        r.op("tensor", "matmul", TP[:, tb * 8:tb * 8 + 8],
             logits[0:8, tb * 128:(tb + 1) * 128], id32[0:8, 0:8],
             start=(tb == 0), stop=(tb == 15), skip_group_check=True)
    r.inc_last("tensor", T)
    t_ltm = T.v
    r.wait("vector", T, t_ltm)
    r.op("vector", "tensor_copy", ltm[:, :], TP[:, 0:128], inc=[(V, 1)])
    r.wait("scalar", V, V.v)
    r.op("scalar", "activation", eL[:, :], ltm[:, :], AF.Exp, inc=[(S, 1)])
    r.wait("vector", S, S.v)
    for tb in range(16):
        blk = slice(tb * 8, tb * 8 + 8)
        r.op("vector", "reduce_sum", rsum[:, tb:tb + 1], eL[:, blk], axis=AX)
    r.op("vector", "reciprocal", rrec[:, 0:16], rsum[:, 0:16])
    for tb in range(16):
        blk = slice(tb * 8, tb * 8 + 8)
        r.op("vector", "tensor_single_scalar", probs[:, blk], eL[:, blk],
             rrec[:, tb:tb + 1], ALU.mult)
        r.op("vector", "reduce_max", m1p[:, tb:tb + 1], probs[:, blk], axis=AX)
        r.op("vector", "tensor_single_scalar", scr8[:, 0:8], probs[:, blk],
             m1p[:, tb:tb + 1], ALU.is_ge)
        r.op("vector", "scalar_tensor_tensor", maskd[:, blk], scr8[:, 0:8], 2.0,
             probs[:, blk], ALU.mult, ALU.subtract)
        r.op("vector", "tensor_reduce", m2p[:, tb:tb + 1], maskd[:, blk], AX, ALU.min,
             negate=True)
        r.op("vector", "tensor_single_scalar", scr8[:, 0:8], probs[:, blk],
             m2p[:, tb:tb + 1], ALU.is_ge)
        r.op("vector", "tensor_mul", maskd[:, blk], probs[:, blk], scr8[:, 0:8],
             inc=[(V, 1)])
    v_wtm = V.v
    v_gp_evac = 0
    for rnd in range(4):
        r.wait("tensor", V, v_wtm if rnd == 0 else v_gp_evac)
        for q4 in range(4):
            tb = rnd * 4 + q4
            r.op("tensor", "matmul", GP[:, q4 * 128:(q4 + 1) * 128],
                 maskd[:, tb * 8:tb * 8 + 8], id32[:, :],
                 start=(q4 == 0), stop=(q4 == 3), skip_group_check=True)
        r.inc_last("tensor", T)
        r.wait("vector", T, T.v)
        r.op("vector", "tensor_copy", wfm[:, rnd * 512:(rnd + 1) * 512], GP[:, :],
             inc=[(V, 1)])
        v_gp_evac = V.v
    r.op("vector", "tensor_copy", wfmb[:, :], wfm[:, :], inc=[(V, 1)])

    if phases == 15:
        for ctr in (T, V, S):
            r.wait("gpsimd", ctr, ctr.v)
        r.op("gpsimd", "dma_start",
             out_d[0:16, :].rearrange("(a b) c -> a (b c)", b=2),
             wfm[:, :], inc=[(sdbg, 16)])
        r.op("gpsimd", "dma_start",
             out_d[16:32, :].rearrange("(a b) c -> a (b c)", b=2),
             logits[:, :], inc=[(sdbg, 16)])
        r.op("gpsimd", "dma_start",
             out_d[32:48, :].rearrange("a (d c) -> (a d) c", d=8),
             probs[:, :], inc=[(sdbg, 16)])
        r.op("gpsimd", "dma_start",
             out_d[48:64, :].rearrange("a (d c) -> (a d) c", d=8),
             ltm[:, :], inc=[(sdbg, 16)])
        r.op("gpsimd", "dma_start",
             out_d[64:72, :].rearrange("a (d c) -> (a d) c", d=16),
             ghi32[:, :], inc=[(sdbg, 16)])
        r.op("gpsimd", "dma_start",
             out_d[72:80, :].rearrange("a (d c) -> (a d) c", d=16),
             glo[:, :], inc=[(sdbg, 16)])
        r.op("gpsimd", "dma_start",
             out_d[80:88, :].rearrange("a (d c) -> (a d) c", d=16),
             gt32[:, :], inc=[(sdbg, 16)])
        for ctr in all_dma:
            r.wait("gpsimd", ctr, ctr.v)
        r.emit(nc)
        a_sc.close(); ab.close(); top.close(); _lp.__exit__(None, None, None)
        return nc
    if phases <= 1:
        _dbg_out(r, nc, out_d, xn, sdbg, (T, V, S), all_dma)
    r.emit(nc)
    a_sc.close()
    if phases <= 1:
        ab.close(); top.close(); _lp.__exit__(None, None, None)
        return nc

    b_sc = ExitStack()
    wq_sb = b_sc.enter_context(nc.sbuf_tensor("wq_sb", [128, 8192], BF16))
    wo_sb = b_sc.enter_context(nc.sbuf_tensor("wo_sb", [128, 8192], BF16))
    prewo = b_sc.enter_context(nc.sbuf_tensor("prewo", [128, 8192], BF16))
    PT_A = b_sc.enter_context(nc.sbuf_tensor("PT_A", [128, 3072], BF16))
    PT_B = b_sc.enter_context(nc.sbuf_tensor("PT_B", [128, 3072], BF16))
    qraw = b_sc.enter_context(nc.sbuf_tensor("qraw", [128, 1024], BF16))
    qshuf = b_sc.enter_context(nc.sbuf_tensor("qshuf", [128, 1024], BF16))
    Qpair = b_sc.enter_context(nc.sbuf_tensor("Qpair", [128, 1024], BF16))
    Qnext = b_sc.enter_context(nc.sbuf_tensor("Qnext", [128, 1024], BF16))
    rb_sb = b_sc.enter_context(nc.sbuf_tensor("rb_sb", [64, 512], F32))
    rlb = b_sc.enter_context(nc.sbuf_tensor("rlb", [65, 2048], BF16))
    rtmp = b_sc.enter_context(nc.sbuf_tensor("rtmp", [65, 1024], F32))
    zst = b_sc.enter_context(nc.sbuf_tensor("zst", [128, 2048], F32))
    PS_A = b_sc.enter_context(nc.psum_tensor("PS_A", [128, 1024], F32))
    PS_B = b_sc.enter_context(nc.psum_tensor("PS_B", [128, 1024], F32))
    PVA = b_sc.enter_context(nc.psum_tensor("PVA", [128, 1024], F32))
    PVB = b_sc.enter_context(nc.psum_tensor("PVB", [128, 1024], F32))

    r = Rec(SELFS)
    r.auto_chain("vector", True)
    r.op("gpsimd", "dma_start", wq_sb[:, :].rearrange("p (dc c) -> p dc c", dc=8),
         wq_d.rearrange("(dc p) c -> p dc c", p=128), inc=[(gset, 16)])
    r.op("gpsimd", "dma_start", wo_sb[:, :].rearrange("p (dc c) -> p dc c", dc=8),
         wo_d.rearrange("(dc p) c -> p dc c", p=128), inc=[(gset, 16)])
    dg_wqo = gset.v
    r.wait("tensor", sset, ds_setup)
    r.wait("tensor", gset, dg_wqo)

    v_prewo_done = 0
    v_pvA_free = 0
    v_pvB_free = 0
    t_pv_prev = 0
    ds_zw = {}
    z_writes = 0
    for qsh in range(2):
        qo = qsh * 1024
        def emit_qrope_vec(t_q, dst):
            r.wait("vector", T, t_q)
            r.op("vector", "tensor_copy", qraw[:, :], PS_A[:, :], inc=[(V, 1)])
            v_ps = V.v
            r.op("vector", "tensor_copy", qshuf[0:32, :], qraw[32:64, :])
            r.op("vector", "tensor_copy", qshuf[32:64, :], qraw[0:32, :])
            r.op("vector", "tensor_copy", qshuf[64:96, :], qraw[96:128, :])
            r.op("vector", "tensor_copy", qshuf[96:128, :], qraw[64:96, :])
            r.op("vector", "tensor_mul", qraw[:, :], qraw[:, :], cqd[:, qo:qo + 1024])
            r.op("vector", "tensor_mul", qshuf[:, :], qshuf[:, :], sqd[:, qo:qo + 1024])
            r.op("vector", "tensor_add", dst[:, :], qraw[:, :], qshuf[:, :],
                 inc=[(V, 1)])
            return V.v, v_ps

        def emit_qbuild(p, gate_s=None, gate_t=None):
            if gate_s is not None:
                r.wait("tensor", S, gate_s)
            if gate_t is not None:
                r.wait("tensor", T, gate_t)
            if v_prewo_done and gate_s is None:
                r.wait("tensor", V, v_prewo_done)
            for st in range(2):
                for dc in range(DC):
                    r.op("tensor", "matmul", PS_A[:, st * 512:(st + 1) * 512],
                         wq_sb[:, dc * 1024 + p * 128: dc * 1024 + p * 128 + 128],
                         xn[:, dc * OWN + qo + st * 512: dc * OWN + qo + st * 512 + 512],
                         start=(dc == 0), stop=(dc == DC - 1))
            r.inc_last("tensor", T)
            return emit_qrope_vec(T.v, Qpair)

        v_qpair, v_psA_free0 = emit_qbuild(0)
        sA63_prev = 0
        sB63_prev = 0
        for p in range(8):
            hA, hB = PAIRS[p]
            Ksrc = K01 if p < 4 else K23
            VA = Vt[hA // 4]
            VB = Vt[hB // 4]
            sA = {}
            sB = {}
            t_pv = {}
            keepA = KEEP[hA]
            keepB = KEEP[hB]
            kcs = keepB
            r.wait("tensor", V, v_qpair)
            if sB63_prev:
                r.wait("tensor", S, sB63_prev)

            prefix = kcs.index(keepA[0])
            do_prebuild = (p < 7 and prefix >= 3)
            if do_prebuild:
                pre_mms = [(st, dc) for st in range(2) for dc in range(DC)]
                n_slots = max(1, prefix - 2)
                per_slot = -(-len(pre_mms) // n_slots)
            v_qnext = 0
            v_psA_next = 0

            def emit_pv(it):
                kc = kcs[it]
                buf = (it % 3) * 1024
                if it == 0 and v_pvB_free:
                    r.wait("tensor", V, v_pvB_free)
                if kc == keepA[0] and v_pvA_free:
                    r.wait("tensor", V, v_pvA_free)
                r.wait("tensor", S, sB[kc])
                if kc in sA:
                    for st in range(2):
                        r.op("tensor", "matmul", PVA[0:65, st * 512:(st + 1) * 512],
                             VA[:, kc * 65: kc * 65 + 65],
                             PT_A[:, buf + st * 512: buf + st * 512 + 512],
                             start=(kc == keepA[0]), stop=(kc == 63),
                             skip_group_check=True)
                for st in range(2):
                    r.op("tensor", "matmul", PVB[0:65, st * 512:(st + 1) * 512],
                         VB[:, kc * 65: kc * 65 + 65],
                         PT_B[:, buf + st * 512: buf + st * 512 + 512],
                         start=(kc == keepB[0]), stop=(kc == 63),
                         skip_group_check=True)
                r.inc_last("tensor", T)
                t_pv[it] = T.v

            prevA = None
            for it, kc in enumerate(kcs):
                doA = kc in keepA
                if doA:
                    if prevA is not None:
                        r.wait("tensor", S, sA[prevA])
                    elif v_psA_next:
                        r.wait("tensor", V, v_psA_next)
                    elif p == 0:
                        r.wait("tensor", V, v_psA_free0)
                    elif sA63_prev:
                        r.wait("tensor", S, sA63_prev)
                    for st in range(2):
                        r.op("tensor", "matmul", PS_A[:, st * 512:(st + 1) * 512],
                             Ksrc[0:64, kc * 128:(kc + 1) * 128],
                             Qpair[0:64, st * 512:(st + 1) * 512], start=True, stop=True)
                    r.inc_last("tensor", T)
                    t_qkA = T.v
                if it > 0:
                    r.wait("tensor", S, sB[kcs[it - 1]])
                for st in range(2):
                    r.op("tensor", "matmul", PS_B[:, st * 512:(st + 1) * 512],
                         Ksrc[64:128, kc * 128:(kc + 1) * 128],
                         Qpair[64:128, st * 512:(st + 1) * 512], start=True, stop=True)
                r.inc_last("tensor", T)
                t_qkB = T.v
                buf = (it % 3) * 1024
                if doA:
                    r.wait("scalar", T, t_qkA)
                    if it >= 3:
                        r.wait("scalar", T, t_pv[it - 3])
                    elif t_pv_prev:
                        r.wait("scalar", T, t_pv_prev)
                    r.op("scalar", "activation", PT_A[:, buf:buf + 1024], PS_A[:, :], AF.Exp,
                         bias=alibi_sb[:, hA * 64 + kc: hA * 64 + kc + 1], inc=[(S, 1)])
                    sA[kc] = S.v
                    prevA = kc
                r.wait("scalar", T, t_qkB)
                if it >= 3:
                    r.wait("scalar", T, t_pv[it - 3])
                elif t_pv_prev:
                    r.wait("scalar", T, t_pv_prev)
                r.op("scalar", "activation", PT_B[:, buf:buf + 1024], PS_B[:, :], AF.Exp,
                     bias=alibi_sb[:, hB * 64 + kc: hB * 64 + kc + 1], inc=[(S, 1)])
                sB[kc] = S.v
                if do_prebuild and it < n_slots and v_qnext == 0:
                    chunk = pre_mms[it * per_slot:(it + 1) * per_slot]
                    for j, (st, dc) in enumerate(chunk):
                        if it == 0 and j == 0:
                            if p == 0:
                                r.wait("tensor", V, v_psA_free0)
                            elif sA63_prev:
                                r.wait("tensor", S, sA63_prev)
                        r.op("tensor", "matmul", PS_A[:, st * 512:(st + 1) * 512],
                             wq_sb[:, dc * 1024 + (p + 1) * 128:
                                   dc * 1024 + (p + 1) * 128 + 128],
                             xn[:, dc * OWN + qo + st * 512:
                                dc * OWN + qo + st * 512 + 512],
                             start=(dc == 0), stop=(dc == DC - 1),
                             skip_group_check=True)
                    if chunk and (it + 1) * per_slot >= len(pre_mms):
                        r.inc_last("tensor", T)
                        v_qnext, v_psA_next = emit_qrope_vec(T.v, Qnext)
                if it >= 1:
                    emit_pv(it - 1)
            emit_pv(len(kcs) - 1)
            if do_prebuild:
                r.wait("vector", T, t_qkB)
                r.op("vector", "tensor_copy", Qpair[:, :], Qnext[:, :], inc=[(V, 1)])
                v_qpair = V.v
            r.wait("scalar", T, t_pv[len(kcs) - 1])
            r.op("scalar", "activation", rtmp[64:65, 0:1024], PVB[64:65, :], AF.Ln)
            r.chain("scalar")
            r.op("scalar", "activation", rlb[64:65, 1024:2048], rtmp[64:65, 0:1024],
                 AF.Exp, scale=-1.0, inc=[(S, 1)])
            s_recB = S.v
            r.chain("scalar")
            r.op("scalar", "activation", rtmp[64:65, 0:1024], PVA[64:65, :], AF.Ln)
            r.chain("scalar")
            r.op("scalar", "activation", rlb[64:65, 0:1024], rtmp[64:65, 0:1024],
                 AF.Exp, scale=-1.0, inc=[(S, 1)])
            s_recA = S.v
            if p < 7 and not do_prebuild:
                v_qpair, v_psA_next = emit_qbuild(p + 1, gate_s=sB[63], gate_t=t_qkB)
            for head, PVx, hh, s_rec in ((1, PVB, hB, s_recB), (0, PVA, hA, s_recA)):
                r.wait("tensor", S, s_rec)
                for st in range(2):
                    r.op("tensor", "matmul", PVx[64:128, st * 512:(st + 1) * 512],
                         ones64[64:65, 0:64],
                         rlb[64:65, head * 1024 + st * 512: head * 1024 + (st + 1) * 512],
                         start=True, stop=True, skip_group_check=True)
                r.inc_last("tensor", T)
                t_bc = T.v
                r.wait("vector", T, t_bc)
                ro = (hh % 2) * 64
                co = (hh // 2) * 1024
                for st in range(2):
                    r.op("vector", "tensor_copy", rb_sb[:, :],
                         PVx[64:128, st * 512:(st + 1) * 512])
                    r.op("vector", "tensor_mul",
                         prewo[ro:ro + 64, co + st * 512:co + (st + 1) * 512],
                         PVx[0:64, st * 512:(st + 1) * 512], rb_sb[:, :],
                         inc=[(V, 1)])
                if head == 1:
                    v_pvB_free = V.v
                else:
                    v_pvA_free = V.v
            v_prewo_done = V.v
            t_pv_prev = t_pv[len(kcs) - 1]
            sA63_prev = sA[63]
            sB63_prev = sB[63]

        r.wait("tensor", V, v_prewo_done)
        for oc in range(DC):
            PSx = PS_A if (oc % 2 == 0) else PS_B
            for st in range(2):
                for ic in range(DC):
                    r.op("tensor", "matmul", PSx[:, st * 512:(st + 1) * 512],
                         wo_sb[:, ic * 1024 + oc * 128: ic * 1024 + oc * 128 + 128],
                         prewo[:, ic * 1024 + st * 512: ic * 1024 + st * 512 + 512],
                         start=(ic == 0), stop=(ic == DC - 1))
            r.inc_last("tensor", T)
            t_wo = T.v
            zb = (oc % 2) * 1024
            r.wait("vector", T, t_wo)
            szp = sz[z_writes % 2]
            if z_writes >= 2:
                r.wait("vector", sz[z_writes % 2], ds_zw[z_writes - 2])
            r.op("vector", "tensor_copy", zst[:, zb:zb + 1024], PSx[:, :], inc=[(V, 1)])
            v_ev = V.v
            r.wait("sync", V, v_ev)
            r.op("sync", "dma_start", z_d[oc, :, qo:qo + 1024], zst[:, zb:zb + 1024],
                 inc=[(szp, 16)])
            ds_zw[z_writes] = szp.v
            z_writes += 1
            if oc + 2 < DC:
                r.wait("tensor", V, v_ev)
        v_prewo_done = V.v
    sz_fin = (sz[0].v, sz[1].v)

    if phases <= 2:
        _dbg_z(r, nc, out_d, z_d, sdbg, (T, V, S), all_dma)
    r.emit(nc)
    b_sc.close()
    ab.close()
    if phases <= 2:
        top.close(); _lp.__exit__(None, None, None)
        return nc

    c_sc = ExitStack()
    sel_sb = c_sc.enter_context(nc.sbuf_tensor("sel_sb", [8, 1024], BF16))
    w1sb = c_sc.enter_context(nc.sbuf_tensor("w1sb", [128, 2 * 12288], BF16))
    w2sb = c_sc.enter_context(nc.sbuf_tensor("w2sb", [128, 2 * 6144], BF16))
    Abuf = c_sc.enter_context(nc.sbuf_tensor("Abuf", [128, 6 * 2048], BF16))
    gact = c_sc.enter_context(nc.sbuf_tensor("gact", [128, 2048], F32))
    wbc = c_sc.enter_context(nc.sbuf_tensor("wbc", [128, 2048], BF16))
    atmp = c_sc.enter_context(nc.sbuf_tensor("atmp", [128, 1024], BF16))
    zst3 = c_sc.enter_context(nc.sbuf_tensor("zst3", [128, 4096], F32))
    PA1 = c_sc.enter_context(nc.psum_tensor("PA1", [128, 1024], F32))
    PA2 = c_sc.enter_context(nc.psum_tensor("PA2", [128, 1024], F32))
    PB1 = c_sc.enter_context(nc.psum_tensor("PB1", [128, 1024], F32))
    PB2 = c_sc.enter_context(nc.psum_tensor("PB2", [128, 1024], F32))

    r = Rec(SELFS)
    r.auto_chain("vector", True)
    r.auto_chain("scalar", True)
    _load_table(r, SIL)
    r.op("gpsimd", "dma_start", sel_sb[:, :], sel_d, inc=[(gset, 16)])
    gset_sel = gset.v
    r.wait("gpsimd", sz[0], sz_fin[0])
    r.wait("gpsimd", sz[1], sz_fin[1])

    dg_w = {}
    for e in range(2):
        buf = e % 2
        r.op("gpsimd", "dma_start",
             w1sb[:, buf * 12288:(buf + 1) * 12288].rearrange("p (dc f) -> p dc f", dc=DC),
             w1_d[e].rearrange("(dc p) f -> p dc f", p=128), inc=[(gw[buf], 16)])
        r.op("gpsimd", "dma_start",
             w2sb[:, buf * 6144:(buf + 1) * 6144].rearrange("p (hc c) -> p hc c", hc=6),
             w2_d[e].rearrange("(hc p) c -> p hc c", p=128), inc=[(gw[buf], 16)])
        dg_w[e] = gw[buf].v

    v_A_done = {}
    v_z3evac = {}
    dg_acc = {}
    t_h1_done = {}
    t_eo_done = {}
    v_last_eoevac = 0
    for e in range(NE):
        buf = e % 2
        r.wait("tensor", gw[buf], dg_w[e])
        if v_last_eoevac:
            r.wait("tensor", V, v_last_eoevac)
        if e < 8:
            if e == 0:
                r.wait("tensor", gset, gset_sel)
            for st in range(4):
                Px = (PA1, PA2, PB1, PB2)[st]
                r.op("tensor", "matmul", Px[:, 0:512],
                     sel_sb[0:8, e * 128:(e + 1) * 128], wfmb[0:8, st * 512:(st + 1) * 512],
                     start=True, stop=True)
            r.inc_last("tensor", T)
            t_wb = T.v
            r.wait("vector", T, t_wb)
            for st in range(4):
                Px = (PA1, PA2, PB1, PB2)[st]
                r.op("vector", "tensor_copy", wbc[:, st * 512:(st + 1) * 512],
                     Px[:, 0:512], inc=[(V, 1)])
            v_wbc = V.v
            r.wait("tensor", V, v_wbc)
        for j in range(6):
            for half in range(2):
                P1 = PA1 if half == 0 else PB1
                P2 = PA2 if half == 0 else PB2
                if (j - 1, half) in v_A_done:
                    r.wait("tensor", V, v_A_done[(j - 1, half)])
                for fcj, Px in ((j, P1), (j + 6, P2)):
                    for st in range(2):
                        for dc in range(DC):
                            r.op("tensor", "matmul", Px[:, st * 512:(st + 1) * 512],
                                 w1sb[:, buf * 12288 + dc * 1536 + fcj * 128:
                                      buf * 12288 + dc * 1536 + fcj * 128 + 128],
                                 xn[:, dc * OWN + half * 1024 + st * 512:
                                    dc * OWN + half * 1024 + st * 512 + 512],
                                 start=(dc == 0), stop=(dc == DC - 1))
                r.inc_last("tensor", T)
                t_h1 = T.v
                r.wait("scalar", T, t_h1)
                r.op("scalar", "activation", gact[:, half * 1024:(half + 1) * 1024],
                     P2[:, :], AF.Silu, inc=[(S, 1)])
                s_si = S.v
                r.wait("vector", S, s_si)
                acol = j * 2048 + half * 1024
                if e < 8:
                    r.op("vector", "tensor_mul", atmp[:, :], P1[:, :],
                         gact[:, half * 1024:(half + 1) * 1024])
                    r.op("vector", "tensor_mul", Abuf[:, acol:acol + 1024], atmp[:, :],
                         wbc[:, half * 1024:(half + 1) * 1024], inc=[(V, 1)])
                else:
                    r.op("vector", "tensor_mul", Abuf[:, acol:acol + 1024], P1[:, :],
                         gact[:, half * 1024:(half + 1) * 1024], inc=[(V, 1)])
                v_A_done[(j, half)] = V.v
        v_A_done.pop((-1, 0), None)
        t_h1_done[e] = T.v
        if e + 2 < NE:
            r.wait("gpsimd", T, t_h1_done[e])
            r.op("gpsimd", "dma_start",
                 w1sb[:, buf * 12288:(buf + 1) * 12288].rearrange("p (dc f) -> p dc f", dc=DC),
                 w1_d[e + 2].rearrange("(dc p) f -> p dc f", p=128), inc=[(gw[buf], 16)])
        r.wait("tensor", V, v_A_done[(5, 1)])
        eo_idx = 0
        v_eo_evac = {}
        for dc in range(DC):
            zb = (dc % 2) * 2048
            if dc >= 2:
                r.wait("vector", ga[dc % 2], dg_acc[(e, dc - 2)])
            elif e > 0:
                r.wait("vector", ga[dc % 2], dg_acc[(e - 1, dc + 6)])
            for half in range(2):
                P1 = PA1 if (eo_idx % 2 == 0) else PB1
                if eo_idx >= 2:
                    r.wait("tensor", V, v_eo_evac[eo_idx - 2])
                for hc in range(6):
                    for st in range(2):
                        r.op("tensor", "matmul", P1[:, st * 512:(st + 1) * 512],
                             w2sb[:, buf * 6144 + hc * 1024 + dc * 128:
                                  buf * 6144 + hc * 1024 + dc * 128 + 128],
                             Abuf[:, hc * 2048 + half * 1024 + st * 512:
                                  hc * 2048 + half * 1024 + st * 512 + 512],
                             start=(hc == 0), stop=(hc == 5))
                r.inc_last("tensor", T)
                t_eo = T.v
                r.wait("vector", T, t_eo)
                r.op("vector", "tensor_copy",
                     zst3[:, zb + half * 1024: zb + half * 1024 + 1024], P1[:, :],
                     inc=[(V, 1)])
                v_eo_evac[eo_idx] = V.v
                eo_idx += 1
            v_z3evac[(e, dc)] = V.v
            r.wait("gpsimd", V, V.v)
            r.op("gpsimd", "dma_start", z_d[dc, :, :], zst3[:, zb:zb + 2048],
                 accum_op=ALU.add, inc=[(ga[dc % 2], 16)])
            dg_acc[(e, dc)] = ga[dc % 2].v
        t_eo_done[e] = T.v
        v_last_eoevac = V.v
        if e + 2 < NE:
            r.wait("gpsimd", T, t_eo_done[e])
            r.op("gpsimd", "dma_start",
                 w2sb[:, buf * 6144:(buf + 1) * 6144].rearrange("p (hc c) -> p hc c", hc=6),
                 w2_d[e + 2].rearrange("(hc p) c -> p hc c", p=128), inc=[(gw[buf], 16)])
            dg_w[e + 2] = gw[buf].v
        v_A_done = {(5, 0): v_A_done[(5, 0)], (5, 1): v_A_done[(5, 1)]}
        v_A_done = {(-1, 0): v_A_done[(5, 0)], (-1, 1): v_A_done[(5, 1)]}
    ga_fin = (ga[0].v, ga[1].v)

    if phases <= 3:
        _dbg_z(r, nc, out_d, z_d, sdbg, (T, V, S), all_dma)
    r.emit(nc)
    c_sc.close()
    if phases <= 3:
        top.close(); _lp.__exit__(None, None, None)
        return nc

    d_sc = ExitStack()
    zin = d_sc.enter_context(nc.sbuf_tensor("zin", [128, 2048], F32))
    xtb = d_sc.enter_context(nc.sbuf_tensor("xtb", [128, 2048], F32))
    zhi = d_sc.enter_context(nc.sbuf_tensor("zhi", [128, 2048], BF16))
    zhi32 = d_sc.enter_context(nc.sbuf_tensor("zhi32", [128, 2048], F32))
    zlo = d_sc.enter_context(nc.sbuf_tensor("zlo", [128, 2048], BF16))
    osb = d_sc.enter_context(nc.sbuf_tensor("osb", [128, 2048], F32))
    PO1 = d_sc.enter_context(nc.psum_tensor("PO1", [128, 1024], F32))
    PO2 = d_sc.enter_context(nc.psum_tensor("PO2", [128, 1024], F32))

    r = Rec(SELFS)
    r.auto_chain("vector", True)
    r.wait("sync", ga[0], ga_fin[0])
    r.wait("sync", ga[1], ga_fin[1])
    z_r = z_d.rearrange("dc p t -> p dc t")
    v_used = {}
    t_tr4 = {}
    ds_out = {}
    for tb in range(16):
        bb = (tb % 2) * 1024
        s4p = s4[tb % 2]
        if tb >= 2:
            r.wait("sync", V, v_used[tb - 2])
        r.op("sync", "dma_start", zin[:, bb:bb + 1024].rearrange("p (dc c) -> p dc c", dc=DC),
             z_r[:, :, tb * 128:(tb + 1) * 128], inc=[(s4p, 16)])
        r.op("sync", "dma_start", xtb[:, bb:bb + 1024], xo_r[:, tb, :], inc=[(s4p, 16)])
        ds_in = s4p.v
        r.wait("vector", s4p, ds_in)
        if tb >= 2:
            r.wait("vector", T, t_tr4[tb - 2])
            r.wait("vector", so[tb % 2], ds_out[tb - 2])
        r.op("vector", "tensor_copy", zhi[:, bb:bb + 1024], zin[:, bb:bb + 1024])
        r.op("vector", "tensor_copy", zhi32[:, bb:bb + 1024], zhi[:, bb:bb + 1024])
        r.op("vector", "tensor_sub", zlo[:, bb:bb + 1024], zin[:, bb:bb + 1024],
             zhi32[:, bb:bb + 1024], inc=[(V, 1)])
        v_hl = V.v
        POx = PO1 if tb % 2 == 0 else PO2
        r.wait("tensor", V, v_hl)
        for dc in range(DC):
            for srci, srcm in ((0, zhi), (1, zlo)):
                r.op("tensor", "matmul", POx[:, dc * 128:(dc + 1) * 128],
                     srcm[:, bb + dc * 128: bb + (dc + 1) * 128], id_sb[:, :],
                     start=(dc % 4 == 0 and srci == 0), stop=(dc % 4 == 3 and srci == 1),
                     skip_group_check=True)
        r.inc_last("tensor", T)
        t_tr4[tb] = T.v
        r.wait("vector", T, T.v)
        r.op("vector", "tensor_add", osb[:, bb:bb + 1024], xtb[:, bb:bb + 1024], POx[:, :],
             inc=[(V, 1)])
        v_used[tb] = V.v
        r.wait("sync", V, V.v)
        r.op("sync", "dma_start", out_d[tb * 128:(tb + 1) * 128, :], osb[:, bb:bb + 1024],
             inc=[(so[tb % 2], 16)])
        ds_out[tb] = so[tb % 2].v
    for ctr in all_dma:
        r.wait("sync", ctr, ctr.v)
    r.emit(nc)
    d_sc.close()
    top.close()
    _lp.__exit__(None, None, None)
    return nc


def _dbg_z(r, nc, out_d, z_d, sdbg, ctrs, all_dma):
    for ctr in ctrs:
        r.wait("gpsimd", ctr, ctr.v)
    for ctr in all_dma:
        r.wait("gpsimd", ctr, ctr.v)
    r.op("gpsimd", "dma_start", out_d.rearrange("(dc p) t -> dc p t", p=128),
         z_d, inc=[(sdbg, 16)])
    r.wait("gpsimd", sdbg, sdbg.v)


def _dbg_out(r, nc, out_d, xn, sdbg, ctrs, all_dma):
    for ctr in ctrs:
        r.wait("gpsimd", ctr, ctr.v)
    r.op("gpsimd", "dma_start", out_d.rearrange("(a p) c -> p a c", p=128),
         xn[:, :].rearrange("p (a c) -> p a c", a=16), inc=[(sdbg, 16)])
    for ctr in all_dma:
        r.wait("gpsimd", ctr, ctr.v)


_NC = None


def _host_tables(qs):
    pos = np.arange(L)
    invf = (1.0 / ROPE_BASE ** (np.arange(0, HD, 2) / HD)).astype(np.float64)
    ang = pos[:, None].astype(np.float64) * invf[None, :]
    cos_t = np.cos(ang).astype(np.float32)
    sin_t = np.sin(ang).astype(np.float32)
    cos64 = np.concatenate([cos_t.T, cos_t.T], axis=0)
    sin64 = np.concatenate([-sin_t.T, sin_t.T], axis=0)
    cosk = np.concatenate([cos64, cos64], axis=0)
    sink = np.concatenate([sin64, sin64], axis=0)
    cosq = cosk[:, qs * OWN:(qs + 1) * OWN] * 0.125
    sinq = sink[:, qs * OWN:(qs + 1) * OWN] * 0.125
    slopes = np.asarray(_alibi_slopes(H), np.float32)
    alibi = np.zeros((128, 1024), np.float32)
    for h in range(H):
        for kc in range(64):
            j = pos[kc * 128:(kc + 1) * 128].astype(np.float32)
            alibi[:, h * 64 + kc] = slopes[h] * (j - (L - 1))
    return (cosk.astype(ml_dtypes.bfloat16), sink.astype(ml_dtypes.bfloat16),
            cosq.astype(ml_dtypes.bfloat16), sinq.astype(ml_dtypes.bfloat16), alibi)


def prep_in_maps(L0_attn_x, L0_attn_Wq, L0_attn_Wk, L0_attn_Wv, L0_attn_Wo,
                 L0_ffn_gate_w, L0_ffn_W1, L0_ffn_W2, L0_ffn_Ws1, L0_ffn_Ws2):
    x = np.ascontiguousarray(np.asarray(L0_attn_x, np.float32))
    Wq = np.asarray(L0_attn_Wq, np.float32)
    Wk = np.ascontiguousarray(np.asarray(L0_attn_Wk, np.float32))
    Wv = np.ascontiguousarray(np.asarray(L0_attn_Wv, np.float32))
    Wo = np.ascontiguousarray(np.asarray(L0_attn_Wo, np.float32))
    gw = np.asarray(L0_ffn_gate_w, np.float32)
    W1 = np.asarray(L0_ffn_W1, np.float32)
    W2 = np.asarray(L0_ffn_W2, np.float32)
    Ws1 = np.asarray(L0_ffn_Ws1, np.float32)
    Ws2 = np.asarray(L0_ffn_Ws2, np.float32)

    colperm = np.concatenate([np.arange(h * 64, (h + 1) * 64)
                              for pr in PAIRS for h in pr])
    wqsh = np.ascontiguousarray(Wq[:, colperm])
    gt = np.ascontiguousarray(gw.T)
    w1t = np.empty((NE, D, 1536), np.float32)
    w2t = np.empty((NE, 768, D), np.float32)
    for e in range(8):
        w1t[e] = W1[e].T
        w2t[e] = W2[e].T
    for sh in range(2):
        w1t[8 + sh] = np.concatenate([Ws1[sh * 768:(sh + 1) * 768],
                                      Ws1[1536 + sh * 768: 1536 + (sh + 1) * 768]],
                                     axis=0).T
        w2t[8 + sh] = Ws2.T[sh * 768:(sh + 1) * 768]
    w1t = np.ascontiguousarray(w1t)
    w2t = np.ascontiguousarray(w2t)
    ident = np.eye(128, dtype=np.float32).astype(ml_dtypes.bfloat16)
    sel = np.zeros((8, 1024), np.float32)
    for e in range(8):
        sel[e, e * 128:(e + 1) * 128] = 1.0
    sel = sel.astype(ml_dtypes.bfloat16)

    in_maps = []
    for c in range(8):
        b, qs = c // 4, c % 4
        cosk, sink, cosq, sinq, alibi = _host_tables(qs)
        xb = np.ascontiguousarray(x[b])
        xown = np.ascontiguousarray(x[b, qs * OWN:(qs + 1) * OWN])
        in_maps.append({
            "xb": xb, "xown": xown, "wqsh": wqsh, "wk": Wk, "wv": Wv, "wo": Wo, "gt": gt,
            "w1t": w1t, "w2t": w2t, "cosk": cosk, "sink": sink, "cosq": cosq,
            "sinq": sinq, "alibi": alibi, "ident": ident, "sel": sel,
        })
    return in_maps


_NCS = {}


def kernel(**inputs):
    import os
    phases = int(os.environ.get("KMK_PHASES", "4"))
    nonce0 = int(os.environ.get("KMK_NONCE", "0"))
    in_maps = prep_in_maps(**inputs)
    ncores = int(os.environ.get("KMK_CORES", "8"))
    in_maps = in_maps[:ncores]
    res = None
    last = None
    for k in range(nonce0, nonce0 + 6):
        key = (phases, k)
        if key not in _NCS:
            _NCS[key] = build_nc(phases=phases, nonce=k,
                                 stripes=int(os.environ.get("KMK_STRIPES", "16")))
        try:
            res = run_bass_kernel_spmd(_NCS[key], in_maps, core_ids=list(range(ncores)))
            break
        except Exception as e:
            last = e
            print(f"[kernel] launch attempt nonce={k} failed: {type(e).__name__}; retrying", flush=True)
    if res is None:
        raise last
    out = np.zeros((B, L, D), np.float32)
    for c in range(ncores):
        b, qs = c // 4, c % 4
        out[b, qs * OWN:(qs + 1) * OWN, :] = res.results[c]["out"]
    return out


# version 14
# speedup vs baseline: 1.0721x; 1.0189x over previous
import numpy as np
import ml_dtypes
from contextlib import ExitStack

import concourse.bass as bass
import concourse.mybir as mybir
from concourse.bass_utils import run_bass_kernel_spmd

F32 = mybir.dt.float32
BF16 = mybir.dt.bfloat16
AF = mybir.ActivationFunctionType
ALU = mybir.AluOpType
AX = mybir.AxisListType.X

B, L, D = 2, 8192, 1024
H, KV, HD = 16, 4, 64
OWN = 2048
DC = 8
NE = 10
PAIRS = [(0, 4), (1, 5), (2, 6), (3, 7), (8, 12), (9, 13), (10, 14), (11, 15)]
ROPE_BASE = 10000.0
ALIBI_SKIP_T = 170.0


def _keep_chunks():
    slopes = _alibi_slopes(H)
    keep = []
    for h in range(H):
        ks = [kc for kc in range(64)
              if slopes[h] * ((L - 1) - (kc * 128 + 127)) <= ALIBI_SKIP_T]
        keep.append(sorted(ks))
    return keep


def _alibi_slopes(n_heads):
    import math
    start = 2.0 ** (-2.0 ** (-(math.log2(n_heads) - 3)))
    return [start * start ** i for i in range(n_heads)]


class Ctr:
    def __init__(self, es, nc, name):
        self.sem = es.enter_context(nc.semaphore(name))
        self.v = 0


class Rec:
    ENGS = ("tensor", "vector", "scalar", "sync", "gpsimd")

    def __init__(self, selfsems=None):
        self.q = {e: [] for e in self.ENGS}
        self.last = {}
        self.lastinfo = {}
        self.selfsems = selfsems or {}
        self.autoc = {}

    def op(self, eng, method, *a, inc=None, **kw):
        incs = []
        if inc:
            for ctr, k in inc:
                ctr.v += k
                incs.append((ctr.sem, k))
        if self.autoc.get(eng):
            self.chain(eng)
        cell = {}
        def f(e, m=method, a=a, kw=kw, incs=tuple(incs), cell=cell):
            ins = getattr(e, m)(*a, **kw)
            cell["i"] = ins
            for s, k in incs:
                ins.then_inc(s, k)
        self.q[eng].append(f)
        self.last[eng] = cell
        self.lastinfo[eng] = (inc[0][0], inc[0][0].v) if inc else None

    def inc_last(self, eng, ctr, k=1):
        ctr.v += k
        cell = self.last[eng]
        self.q[eng].append(lambda e, c=cell, s=ctr.sem, kk=k: c["i"].then_inc(s, kk))
        self.lastinfo[eng] = (ctr, ctr.v)

    def chain(self, eng):
        info = self.lastinfo.get(eng)
        if info is None:
            if eng not in self.last:
                return
            ctr = self.selfsems[eng]
            self.inc_last(eng, ctr)
            info = (ctr, ctr.v)
        ctr, v = info
        self.q[eng].append(lambda e, s=ctr.sem, n=v: e.wait_ge(s, n))
        self.lastinfo[eng] = None

    def auto_chain(self, eng, on):
        self.autoc[eng] = on

    def wait(self, eng, ctr, v=None):
        vv = ctr.v if v is None else v
        if vv <= 0:
            return
        self.q[eng].append(lambda e, s=ctr.sem, n=vv: e.wait_ge(s, n))

    def raw(self, eng, fn):
        self.q[eng].append(fn)

    def emit(self, nc):
        with nc.Block() as blk:
            for name in self.ENGS:
                ops = self.q[name]
                if not ops:
                    continue
                def body(e, ops=ops):
                    for f in ops:
                        f(e)
                getattr(blk, name)(body)


def _act_set_id(nc, set_name):
    from concourse.hw_specs import get_activation_tables
    names = list(get_activation_tables(nc.m.arch).keys())
    return names.index(set_name)


def _load_table(rec, set_id):
    def f(e, idx=set_id):
        e.add_instruction(mybir.InstLoadActFuncSet(
            name=e.bass.get_next_instruction_name(),
            act_func_set_id=idx, ins=[], outs=[]))
    rec.raw("scalar", f)


def build_nc(phases=4, nonce=0, stripes=16):
    import os as _os
    CUT = set(_os.environ.get("KMK_CUT", "").split(","))
    nc = bass.Bass("TRN2", target_bir_lowering=False)
    _lp = nc.allow_low_precision("bf16 elementwise outputs feed bf16 matmul operands by design")
    _lp.__enter__()

    xp_d = nc.dram_tensor("xb", [L, D], F32, kind="ExternalInput").ap()
    xo_d = nc.dram_tensor("xown", [OWN, D], F32, kind="ExternalInput").ap()
    wq_d = nc.dram_tensor("wqsh", [D, D], F32, kind="ExternalInput").ap()
    wk_d = nc.dram_tensor("wk", [D, 256], F32, kind="ExternalInput").ap()
    wv_d = nc.dram_tensor("wv", [D, 256], F32, kind="ExternalInput").ap()
    wo_d = nc.dram_tensor("wo", [D, D], F32, kind="ExternalInput").ap()
    gt_d = nc.dram_tensor("gt", [D, 8], F32, kind="ExternalInput").ap()
    w1_d = nc.dram_tensor("w1t", [NE, D, 1536], F32, kind="ExternalInput").ap()
    w2_d = nc.dram_tensor("w2t", [NE, 768, D], F32, kind="ExternalInput").ap()
    cosk_d = nc.dram_tensor("cosk", [128, L], BF16, kind="ExternalInput").ap()
    sink_d = nc.dram_tensor("sink", [128, L], BF16, kind="ExternalInput").ap()
    cosq_d = nc.dram_tensor("cosq", [128, OWN], BF16, kind="ExternalInput").ap()
    sinq_d = nc.dram_tensor("sinq", [128, OWN], BF16, kind="ExternalInput").ap()
    alibi_d = nc.dram_tensor("alibi", [128, 1024], F32, kind="ExternalInput").ap()
    id_d = nc.dram_tensor("ident", [128, 128], BF16, kind="ExternalInput").ap()
    sel_d = nc.dram_tensor("sel", [8, 1024], BF16, kind="ExternalInput").ap()
    out_d = nc.dram_tensor("out", [OWN, D], F32, kind="ExternalOutput").ap()
    z_d = nc.dram_tensor("zacc", [DC, 128, OWN], F32).ap()

    KEEP = _keep_chunks()
    NLE = _act_set_id(nc, "natural_log_exp_and_others")
    SIL = _act_set_id(nc, "silu_and_others")

    top = ExitStack()
    xn = top.enter_context(nc.sbuf_tensor("xn", [128, DC * OWN], BF16))
    wfmb = top.enter_context(nc.sbuf_tensor("wfmb", [8, OWN], BF16))
    ones64 = top.enter_context(nc.sbuf_tensor("ones64", [65, 128], BF16))
    id_sb = top.enter_context(nc.sbuf_tensor("id_sb", [128, 128], BF16))


    ab = ExitStack()
    wfm = ab.enter_context(nc.sbuf_tensor("wfm", [8, OWN], F32))
    alibi_sb = ab.enter_context(nc.sbuf_tensor("alibi_sb", [128, 1024], F32))
    cqd = ab.enter_context(nc.sbuf_tensor("cqd", [128, OWN], BF16))
    sqd = ab.enter_context(nc.sbuf_tensor("sqd", [128, OWN], BF16))
    K01 = ab.enter_context(nc.sbuf_tensor("K01", [128, L], BF16))
    K23 = ab.enter_context(nc.sbuf_tensor("K23", [128, L], BF16))
    Vt = [ab.enter_context(nc.sbuf_tensor(f"V{k}", [128, 64 * 65], BF16)) for k in range(KV)]

    sset = Ctr(top, nc, "sset")
    gset = Ctr(top, nc, "gset")
    sx = [Ctr(top, nc, "sx0"), Ctr(top, nc, "sx1")]
    scs = Ctr(top, nc, "scs")
    sz = [Ctr(top, nc, "sz0"), Ctr(top, nc, "sz1")]
    gw = [Ctr(top, nc, "gw0"), Ctr(top, nc, "gw1")]
    ga = [Ctr(top, nc, "ga0"), Ctr(top, nc, "ga1")]
    s4 = [Ctr(top, nc, "s4a"), Ctr(top, nc, "s4b")]
    so = [Ctr(top, nc, "so0"), Ctr(top, nc, "so1")]
    sdbg = Ctr(top, nc, "sdbg")
    all_dma = [sset, gset, scs, sdbg] + sx + sz + gw + ga + s4 + so
    T = Ctr(top, nc, "T")
    V = Ctr(top, nc, "Vv")
    S = Ctr(top, nc, "S")
    VSF = Ctr(top, nc, "VSF")
    SSF = Ctr(top, nc, "SSF")
    SELFS = {"vector": VSF, "scalar": SSF}

    a_sc = ExitStack()
    wk_sb = a_sc.enter_context(nc.sbuf_tensor("wk_sb", [128, 2048], BF16))
    wv_sb = a_sc.enter_context(nc.sbuf_tensor("wv_sb", [128, 2048], BF16))
    gt32 = a_sc.enter_context(nc.sbuf_tensor("gt32", [128, 64], F32))
    ghi = a_sc.enter_context(nc.sbuf_tensor("ghi", [128, 64], BF16))
    ghi32 = a_sc.enter_context(nc.sbuf_tensor("ghi32", [128, 64], F32))
    glo = a_sc.enter_context(nc.sbuf_tensor("glo", [128, 64], BF16))
    coskS = a_sc.enter_context(nc.sbuf_tensor("coskS", [128, 512], BF16))
    sinkS = a_sc.enter_context(nc.sbuf_tensor("sinkS", [128, 512], BF16))
    xt = a_sc.enter_context(nc.sbuf_tensor("xt", [128, 2048], F32))
    xn32 = a_sc.enter_context(nc.sbuf_tensor("xn32", [128, 1024], F32))
    xnhi32 = a_sc.enter_context(nc.sbuf_tensor("xnhi32", [128, 1024], F32))
    xntm = a_sc.enter_context(nc.sbuf_tensor("xntm", [128, 2048], BF16))
    lotm = a_sc.enter_context(nc.sbuf_tensor("lotm", [128, 2048], BF16))
    fmstripe = a_sc.enter_context(nc.sbuf_tensor("fmstripe", [128, 4096], BF16))
    kraw = a_sc.enter_context(nc.sbuf_tensor("kraw", [128, 512], BF16))
    kshuf = a_sc.enter_context(nc.sbuf_tensor("kshuf", [128, 512], BF16))
    rt1 = a_sc.enter_context(nc.sbuf_tensor("rt1", [128, 512], BF16))
    rt2 = a_sc.enter_context(nc.sbuf_tensor("rt2", [128, 512], BF16))
    vstage = a_sc.enter_context(nc.sbuf_tensor("vstage", [128, 512], BF16))
    ssum = a_sc.enter_context(nc.sbuf_tensor("ssum", [128, 64], F32))
    qsum = a_sc.enter_context(nc.sbuf_tensor("qsum", [128, 64], F32))
    e2s = a_sc.enter_context(nc.sbuf_tensor("e2s", [128, 64], F32))
    mus = a_sc.enter_context(nc.sbuf_tensor("mus", [128, 64], F32))
    vvs = a_sc.enter_context(nc.sbuf_tensor("vvs", [128, 64], F32))
    lns = a_sc.enter_context(nc.sbuf_tensor("lns", [128, 64], F32))
    rstds = a_sc.enter_context(nc.sbuf_tensor("rstds", [128, 64], F32))
    nbs = a_sc.enter_context(nc.sbuf_tensor("nbs", [128, 64], F32))
    sqscr = a_sc.enter_context(nc.sbuf_tensor("sqscr", [128, 1024], BF16))
    logits = a_sc.enter_context(nc.sbuf_tensor("logits", [8, 2048], F32))
    id32 = a_sc.enter_context(nc.sbuf_tensor("id32", [128, 128], F32))
    eps_sb = a_sc.enter_context(nc.sbuf_tensor("eps_sb", [128, 1], F32))
    ltm = a_sc.enter_context(nc.sbuf_tensor("ltm", [128, 128], F32))
    eL = a_sc.enter_context(nc.sbuf_tensor("eL", [128, 128], F32))
    probs = a_sc.enter_context(nc.sbuf_tensor("probs", [128, 128], F32))
    maskd = a_sc.enter_context(nc.sbuf_tensor("maskd", [128, 128], F32))
    scr8 = a_sc.enter_context(nc.sbuf_tensor("scr8", [128, 8], F32))
    m1p = a_sc.enter_context(nc.sbuf_tensor("m1p", [128, 16], F32))
    m2p = a_sc.enter_context(nc.sbuf_tensor("m2p", [128, 16], F32))
    rsum = a_sc.enter_context(nc.sbuf_tensor("rsum", [128, 16], F32))
    rrec = a_sc.enter_context(nc.sbuf_tensor("rrec", [128, 16], F32))
    TP = a_sc.enter_context(nc.psum_tensor("TP", [128, 1024], F32))
    TP2 = a_sc.enter_context(nc.psum_tensor("TP2", [128, 1024], F32))
    KP = a_sc.enter_context(nc.psum_tensor("KP", [128, 512], F32))
    VP = a_sc.enter_context(nc.psum_tensor("VP", [128, 512], F32))
    VT = a_sc.enter_context(nc.psum_tensor("VT", [128, 512], F32))
    GP = a_sc.enter_context(nc.psum_tensor("GP", [8, 512], F32))

    r = Rec(SELFS)
    r.auto_chain("vector", True)
    r.auto_chain("scalar", True)
    _load_table(r, NLE)
    for _ in range(nonce):
        r.op("tensor", "nop")

    r.op("sync", "dma_start", alibi_sb[:, :], alibi_d, inc=[(sset, 16)])
    r.op("sync", "dma_start", id_sb[:, :], id_d, inc=[(sset, 16)])
    r.op("sync", "dma_start", cqd[:, :], cosq_d, inc=[(sset, 16)])
    r.op("sync", "dma_start", sqd[:, :], sinq_d, inc=[(sset, 16)])
    r.op("sync", "dma_start", gt32[:, :].rearrange("p (dc e) -> p dc e", dc=8),
         gt_d.rearrange("(dc p) e -> p dc e", p=128), inc=[(sset, 16)])
    ds_setup = sset.v
    r.op("gpsimd", "dma_start", wk_sb[:, :].rearrange("p (dc c) -> p dc c", dc=8),
         wk_d.rearrange("(dc p) c -> p dc c", p=128), inc=[(gset, 16)])
    r.op("gpsimd", "dma_start", wv_sb[:, :].rearrange("p (dc c) -> p dc c", dc=8),
         wv_d.rearrange("(dc p) c -> p dc c", p=128), inc=[(gset, 16)])
    dg_w0 = gset.v

    r.op("vector", "memset", ones64[64:65, :], 1.0)
    for k in range(KV):
        r.op("vector", "memset",
             Vt[k][:, :].rearrange("p (kc c) -> p kc c", c=65)[:, :, 64:65], 1.0)
    r.wait("vector", sset, ds_setup)
    r.op("vector", "tensor_copy", id32[:, :], id_sb[:, :])
    r.op("vector", "memset", eps_sb[:, :], 1e-5)
    r.op("vector", "tensor_copy", ghi[:, :], gt32[:, :])
    r.op("vector", "tensor_copy", ghi32[:, :], ghi[:, :])
    r.op("vector", "tensor_sub", glo[:, :], gt32[:, :], ghi32[:, :], inc=[(V, 1)])
    v_ginit = V.v

    xp_r = xp_d.rearrange("(t p) c -> p t c", p=128)
    xo_r = xo_d.rearrange("(t p) c -> p t c", p=128)
    v_xt_done = {}
    s_xt_done = {}
    t_xntm_done = {}
    t_lotm_done = {}
    v_cos_done = {}
    t_kv_done = 0
    t_gate_prev = 0
    v_logit_evac = 0
    v_kevac = 0

    for s in range(stripes + 4):
        own = s >= stripes
        sow = s - stripes
        src_r = xo_r if own else xp_r
        fm_guard = max(t_kv_done, t_gate_prev)
        cb = 0
        for i in range(4):
            tau = 4 * s + i
            lt = (4 * sow + i) if own else tau
            tb = (tau % 2) * 1024
            if tau >= 2:
                r.wait("sync", V, v_xt_done[tau - 2])
                r.wait("sync", S, s_xt_done[tau - 2])
            sxp = sx[tau % 2]
            r.op("sync", "dma_start", xt[:, tb:tb + 1024], src_r[:, lt, :], inc=[(sxp, 16)])
            ds_tile = sxp.v

            r.wait("vector", sxp, ds_tile)
            r.op("vector", "reduce_sum", ssum[:, tau % 64:tau % 64 + 1], xt[:, tb:tb + 1024], axis=AX)
            r.wait("scalar", sxp, ds_tile)
            r.op("scalar", "activation", sqscr[:, :], xt[:, tb:tb + 1024], AF.Square,
                 accum_out=qsum[:, tau % 64:tau % 64 + 1], inc=[(S, 1)])
            s_sq = S.v
            s_xt_done[tau] = s_sq
            c = tau % 64
            r.wait("vector", S, s_sq)
            r.op("vector", "tensor_single_scalar", mus[:, c:c + 1], ssum[:, c:c + 1],
                 1.0 / 1024.0, ALU.mult)
            r.op("vector", "tensor_scalar", vvs[:, c:c + 1], mus[:, c:c + 1],
                 mus[:, c:c + 1], None, ALU.mult)
            r.op("vector", "scalar_tensor_tensor", vvs[:, c:c + 1], qsum[:, c:c + 1],
                 -1.0 / 1024.0, vvs[:, c:c + 1], ALU.mult, ALU.add, inc=[(V, 1)])
            v_vv = V.v
            r.wait("scalar", V, v_vv)
            r.op("scalar", "activation", lns[:, c:c + 1], vvs[:, c:c + 1], AF.Ln,
                 bias=eps_sb[:, 0:1], scale=-1.0)
            r.op("scalar", "activation", rstds[:, c:c + 1], lns[:, c:c + 1], AF.Exp,
                 scale=-0.5, inc=[(S, 1)])
            s_rstd = S.v
            r.wait("vector", S, s_rstd)
            if tau >= 2:
                r.wait("vector", T, t_xntm_done[tau - 2])
            if own:
                r.op("vector", "tensor_scalar", xn32[:, :], xt[:, tb:tb + 1024],
                     mus[:, c:c + 1], rstds[:, c:c + 1], ALU.subtract, ALU.mult)
                r.op("vector", "tensor_copy", xntm[:, tb:tb + 1024], xn32[:, :])
                if tau >= 2 and (tau - 2) in t_lotm_done:
                    r.wait("vector", T, t_lotm_done[tau - 2])
                r.op("vector", "tensor_copy", xnhi32[:, :], xntm[:, tb:tb + 1024])
                r.op("vector", "tensor_sub", lotm[:, tb:tb + 1024], xn32[:, :],
                     xnhi32[:, :], inc=[(V, 1)])
            else:
                r.op("vector", "tensor_scalar", xntm[:, tb:tb + 1024], xt[:, tb:tb + 1024],
                     mus[:, c:c + 1], rstds[:, c:c + 1], ALU.subtract, ALU.mult,
                     inc=[(V, 1)])
            v_xt_done[tau] = V.v
            v_aff = V.v

            r.wait("tensor", V, v_aff)
            TPb = TP if (tau % 2) == 0 else TP2
            for dc in range(DC):
                r.op("tensor", "matmul", TPb[:, dc * 128:dc * 128 + 128],
                     xntm[:, tb + dc * 128:tb + dc * 128 + 128], id_sb[:, :],
                     start=(dc % 4 == 0), stop=(dc % 4 == 3), skip_group_check=True)
            r.inc_last("tensor", T)
            t_hi = T.v
            r.wait("vector", T, t_hi)
            if i == 0:
                r.wait("vector", T, fm_guard)
            if own:
                dst = xn[:, :].rearrange("p (dc t) -> p dc t", dc=DC)[:, :, (4 * sow + i) * 128:(4 * sow + i) * 128 + 128]
            else:
                dst = fmstripe[:, :].rearrange("p (dc t) -> p dc t", dc=DC)[:, :, i * 128:i * 128 + 128]
            r.op("vector", "tensor_copy", dst,
                 TPb[:, :].rearrange("p (dc t) -> p dc t", dc=DC), inc=[(V, 1)])
            v_hi_evac = V.v
            if own:
                r.wait("tensor", V, v_hi_evac)
                for dc in range(DC):
                    r.op("tensor", "matmul", TPb[:, dc * 128:dc * 128 + 128],
                         lotm[:, tb + dc * 128:tb + dc * 128 + 128], id_sb[:, :],
                         start=(dc % 4 == 0), stop=(dc % 4 == 3), skip_group_check=True)
                r.inc_last("tensor", T)
                t_lotm_done[tau] = T.v
                t_xntm_done[tau] = T.v
                r.wait("vector", T, T.v)
                r.op("vector", "tensor_copy",
                     fmstripe[:, :].rearrange("p (dc t) -> p dc t", dc=DC)[:, :, i * 128:i * 128 + 128],
                     TPb[:, :].rearrange("p (dc t) -> p dc t", dc=DC), inc=[(V, 1)])
            else:
                t_xntm_done[tau] = t_hi
        v_stripe = V.v
        if own:
            pass
        else:
            if s >= 1:
                r.wait("sync", V, v_cos_done[s - 1])
            r.op("sync", "dma_start", coskS[:, cb:cb + 512], cosk_d[:, s * 512:(s + 1) * 512],
                 inc=[(scs, 16)])
            r.op("sync", "dma_start", sinkS[:, cb:cb + 512], sink_d[:, s * 512:(s + 1) * 512],
                 inc=[(scs, 16)])
            ds_cos = scs.v

        def xn_src(dc, st=s):
            return fmstripe[:, dc * 512: dc * 512 + 512]

        if "kv" in CUT or own:
            if own and "gate" not in CUT:
                r.wait("tensor", V, v_ginit)
                r.wait("tensor", V, v_stripe)
                if v_logit_evac:
                    r.wait("tensor", V, v_logit_evac)
                first = True
                for gsrc, gw_ in ((0, ghi), (0, glo), (1, ghi)):
                    for dc in range(DC):
                        if gsrc == 0:
                            rhs = xn[:, dc * OWN + sow * 512: dc * OWN + sow * 512 + 512]
                        else:
                            rhs = fmstripe[:, dc * 512: dc * 512 + 512]
                        r.op("tensor", "matmul", GP[:, :], gw_[:, dc * 8:dc * 8 + 8], rhs,
                             start=first, stop=(gsrc == 1 and dc == DC - 1))
                        first = False
                r.inc_last("tensor", T)
                t_gate_prev = T.v
                r.wait("vector", T, t_gate_prev)
                r.op("vector", "tensor_copy", logits[:, sow * 512:(sow + 1) * 512], GP[:, :],
                     inc=[(V, 1)])
                v_logit_evac = V.v
            t_kv_done = t_gate_prev if own else t_kv_done
            continue
        if s == 0:
            r.wait("tensor", gset, dg_w0)
        r.wait("tensor", V, v_stripe)
        for cch in range(1 if "cch1" in CUT else 2):
            Kdst = K01 if cch == 0 else K23
            if v_kevac:
                r.wait("tensor", V, v_kevac)
            for dc in range(DC):
                r.op("tensor", "matmul", KP[:, :],
                     wk_sb[:, dc * 256 + cch * 128: dc * 256 + cch * 128 + 128],
                     xn_src(dc), start=(dc == 0), stop=(dc == DC - 1))
            r.inc_last("tensor", T)
            t_k = T.v
            r.wait("vector", T, t_k)
            r.op("vector", "tensor_copy", kraw[:, :], KP[:, :], inc=[(V, 1)])
            if "rope" in CUT:
                r.op("vector", "tensor_copy", Kdst[:, s * 512:(s + 1) * 512], kraw[:, :])
                skiprope = True
            else:
                skiprope = False
            if not skiprope:
                r.op("vector", "tensor_copy", kshuf[0:32, :], kraw[32:64, :])
                r.op("vector", "tensor_copy", kshuf[32:64, :], kraw[0:32, :])
                r.op("vector", "tensor_copy", kshuf[64:96, :], kraw[96:128, :])
                r.op("vector", "tensor_copy", kshuf[96:128, :], kraw[64:96, :])
                r.wait("vector", scs, ds_cos)
                r.op("vector", "tensor_mul", rt1[:, :], kraw[:, :], coskS[:, cb:cb + 512])
                r.op("vector", "tensor_mul", rt2[:, :], kshuf[:, :], sinkS[:, cb:cb + 512])
                r.op("vector", "tensor_add", Kdst[:, s * 512:(s + 1) * 512], rt1[:, :], rt2[:, :])
            for dc in range(DC):
                r.op("tensor", "matmul", VP[:, :],
                     wv_sb[:, dc * 256 + cch * 128: dc * 256 + cch * 128 + 128],
                     xn_src(dc), start=(dc == 0), stop=(dc == DC - 1))
            r.inc_last("tensor", T)
            t_v = T.v
            r.wait("vector", T, t_v)
            r.op("vector", "tensor_copy", vstage[:, :], VP[:, :], inc=[(V, 1)])
            v_vstage = V.v
            if "vtr" in CUT:
                v_kevac = V.v
                continue
            r.wait("tensor", V, v_vstage)
            for kvl in range(2):
                if kvl == 1:
                    r.wait("tensor", V, V.v)
                for tb4 in range(4):
                    r.op("tensor", "matmul", VT[:, tb4 * 64:tb4 * 64 + 64],
                         vstage[kvl * 64:(kvl + 1) * 64, tb4 * 128:(tb4 + 1) * 128],
                         id_sb[kvl * 64:(kvl + 1) * 64, kvl * 64:kvl * 64 + 64],
                         start=(tb4 == 0), stop=(tb4 == 3), skip_group_check=True)
                r.inc_last("tensor", T)
                r.wait("vector", T, T.v)
                kvi = cch * 2 + kvl
                src = VT[:, 0:256].rearrange("p (t c) -> p t c", t=4)
                dst = Vt[kvi][:, :].rearrange("p (kc c) -> p kc c", c=65)[:, s * 4:s * 4 + 4, 0:64]
                r.op("vector", "tensor_copy", dst, src, inc=[(V, 1)])
            v_kevac = V.v
        v_cos_done[s] = V.v
        t_kv_done = T.v


    if stripes < 4:
        _dbg_out(r, nc, out_d, xn, sdbg, (T, V, S), all_dma)
        r.emit(nc)
        a_sc.close()
        ab.close(); top.close(); _lp.__exit__(None, None, None)
        return nc
    r.wait("tensor", V, V.v)
    for tb in range(16):
        r.op("tensor", "matmul", TP[:, tb * 8:tb * 8 + 8],
             logits[0:8, tb * 128:(tb + 1) * 128], id32[0:8, 0:8],
             start=(tb == 0), stop=(tb == 15), skip_group_check=True)
    r.inc_last("tensor", T)
    t_ltm = T.v
    r.wait("vector", T, t_ltm)
    r.op("vector", "tensor_copy", ltm[:, :], TP[:, 0:128], inc=[(V, 1)])
    r.wait("scalar", V, V.v)
    r.op("scalar", "activation", eL[:, :], ltm[:, :], AF.Exp, inc=[(S, 1)])
    r.wait("vector", S, S.v)
    for tb in range(16):
        blk = slice(tb * 8, tb * 8 + 8)
        r.op("vector", "reduce_sum", rsum[:, tb:tb + 1], eL[:, blk], axis=AX)
    r.op("vector", "reciprocal", rrec[:, 0:16], rsum[:, 0:16])
    for tb in range(16):
        blk = slice(tb * 8, tb * 8 + 8)
        r.op("vector", "tensor_single_scalar", probs[:, blk], eL[:, blk],
             rrec[:, tb:tb + 1], ALU.mult)
        r.op("vector", "reduce_max", m1p[:, tb:tb + 1], probs[:, blk], axis=AX)
        r.op("vector", "tensor_single_scalar", scr8[:, 0:8], probs[:, blk],
             m1p[:, tb:tb + 1], ALU.is_ge)
        r.op("vector", "scalar_tensor_tensor", maskd[:, blk], scr8[:, 0:8], 2.0,
             probs[:, blk], ALU.mult, ALU.subtract)
        r.op("vector", "tensor_reduce", m2p[:, tb:tb + 1], maskd[:, blk], AX, ALU.min,
             negate=True)
        r.op("vector", "tensor_single_scalar", scr8[:, 0:8], probs[:, blk],
             m2p[:, tb:tb + 1], ALU.is_ge)
        r.op("vector", "tensor_mul", maskd[:, blk], probs[:, blk], scr8[:, 0:8],
             inc=[(V, 1)])
    v_wtm = V.v
    v_gp_evac = 0
    for rnd in range(4):
        r.wait("tensor", V, v_wtm if rnd == 0 else v_gp_evac)
        for q4 in range(4):
            tb = rnd * 4 + q4
            r.op("tensor", "matmul", GP[:, q4 * 128:(q4 + 1) * 128],
                 maskd[:, tb * 8:tb * 8 + 8], id32[:, :],
                 start=(q4 == 0), stop=(q4 == 3), skip_group_check=True)
        r.inc_last("tensor", T)
        r.wait("vector", T, T.v)
        r.op("vector", "tensor_copy", wfm[:, rnd * 512:(rnd + 1) * 512], GP[:, :],
             inc=[(V, 1)])
        v_gp_evac = V.v
    r.op("vector", "tensor_copy", wfmb[:, :], wfm[:, :], inc=[(V, 1)])

    if phases == 15:
        for ctr in (T, V, S):
            r.wait("gpsimd", ctr, ctr.v)
        r.op("gpsimd", "dma_start",
             out_d[0:16, :].rearrange("(a b) c -> a (b c)", b=2),
             wfm[:, :], inc=[(sdbg, 16)])
        r.op("gpsimd", "dma_start",
             out_d[16:32, :].rearrange("(a b) c -> a (b c)", b=2),
             logits[:, :], inc=[(sdbg, 16)])
        r.op("gpsimd", "dma_start",
             out_d[32:48, :].rearrange("a (d c) -> (a d) c", d=8),
             probs[:, :], inc=[(sdbg, 16)])
        r.op("gpsimd", "dma_start",
             out_d[48:64, :].rearrange("a (d c) -> (a d) c", d=8),
             ltm[:, :], inc=[(sdbg, 16)])
        r.op("gpsimd", "dma_start",
             out_d[64:72, :].rearrange("a (d c) -> (a d) c", d=16),
             ghi32[:, :], inc=[(sdbg, 16)])
        r.op("gpsimd", "dma_start",
             out_d[72:80, :].rearrange("a (d c) -> (a d) c", d=16),
             glo[:, :], inc=[(sdbg, 16)])
        r.op("gpsimd", "dma_start",
             out_d[80:88, :].rearrange("a (d c) -> (a d) c", d=16),
             gt32[:, :], inc=[(sdbg, 16)])
        for ctr in all_dma:
            r.wait("gpsimd", ctr, ctr.v)
        r.emit(nc)
        a_sc.close(); ab.close(); top.close(); _lp.__exit__(None, None, None)
        return nc
    if phases <= 1:
        _dbg_out(r, nc, out_d, xn, sdbg, (T, V, S), all_dma)
    r.emit(nc)
    a_sc.close()
    if phases <= 1:
        ab.close(); top.close(); _lp.__exit__(None, None, None)
        return nc

    b_sc = ExitStack()
    wq_sb = b_sc.enter_context(nc.sbuf_tensor("wq_sb", [128, 8192], BF16))
    wo_sb = b_sc.enter_context(nc.sbuf_tensor("wo_sb", [128, 8192], BF16))
    prewo = b_sc.enter_context(nc.sbuf_tensor("prewo", [128, 8192], BF16))
    PT_A = b_sc.enter_context(nc.sbuf_tensor("PT_A", [128, 3072], BF16))
    PT_B = b_sc.enter_context(nc.sbuf_tensor("PT_B", [128, 3072], BF16))
    qraw = b_sc.enter_context(nc.sbuf_tensor("qraw", [128, 1024], BF16))
    qshuf = b_sc.enter_context(nc.sbuf_tensor("qshuf", [128, 1024], BF16))
    Qpair = b_sc.enter_context(nc.sbuf_tensor("Qpair", [128, 1024], BF16))
    Qnext = b_sc.enter_context(nc.sbuf_tensor("Qnext", [128, 1024], BF16))
    rb_sb = b_sc.enter_context(nc.sbuf_tensor("rb_sb", [64, 512], F32))
    rlb = b_sc.enter_context(nc.sbuf_tensor("rlb", [65, 2048], BF16))
    rtmp = b_sc.enter_context(nc.sbuf_tensor("rtmp", [65, 1024], F32))
    zst = b_sc.enter_context(nc.sbuf_tensor("zst", [128, 2048], F32))
    PS_A = b_sc.enter_context(nc.psum_tensor("PS_A", [128, 1024], F32))
    PS_B = b_sc.enter_context(nc.psum_tensor("PS_B", [128, 1024], F32))
    PVA = b_sc.enter_context(nc.psum_tensor("PVA", [128, 1024], F32))
    PVB = b_sc.enter_context(nc.psum_tensor("PVB", [128, 1024], F32))

    r = Rec(SELFS)
    r.auto_chain("vector", True)
    r.op("gpsimd", "dma_start", wq_sb[:, :].rearrange("p (dc c) -> p dc c", dc=8),
         wq_d.rearrange("(dc p) c -> p dc c", p=128), inc=[(gset, 16)])
    r.op("gpsimd", "dma_start", wo_sb[:, :].rearrange("p (dc c) -> p dc c", dc=8),
         wo_d.rearrange("(dc p) c -> p dc c", p=128), inc=[(gset, 16)])
    dg_wqo = gset.v
    r.wait("tensor", sset, ds_setup)
    r.wait("tensor", gset, dg_wqo)

    v_prewo_done = 0
    v_pvA_free = 0
    v_pvB_free = 0
    t_pv_prev = 0
    ds_zw = {}
    z_writes = 0
    for qsh in range(2):
        qo = qsh * 1024
        def emit_qrope_vec(t_q, dst):
            r.wait("vector", T, t_q)
            r.op("vector", "tensor_copy", qraw[:, :], PS_A[:, :], inc=[(V, 1)])
            v_ps = V.v
            r.op("vector", "tensor_copy", qshuf[0:32, :], qraw[32:64, :])
            r.op("vector", "tensor_copy", qshuf[32:64, :], qraw[0:32, :])
            r.op("vector", "tensor_copy", qshuf[64:96, :], qraw[96:128, :])
            r.op("vector", "tensor_copy", qshuf[96:128, :], qraw[64:96, :])
            r.op("vector", "tensor_mul", qraw[:, :], qraw[:, :], cqd[:, qo:qo + 1024])
            r.op("vector", "tensor_mul", qshuf[:, :], qshuf[:, :], sqd[:, qo:qo + 1024])
            r.op("vector", "tensor_add", dst[:, :], qraw[:, :], qshuf[:, :],
                 inc=[(V, 1)])
            return V.v, v_ps

        def emit_qbuild(p, gate_s=None, gate_t=None):
            if gate_s is not None:
                r.wait("tensor", S, gate_s)
            if gate_t is not None:
                r.wait("tensor", T, gate_t)
            if v_prewo_done and gate_s is None:
                r.wait("tensor", V, v_prewo_done)
            for st in range(2):
                for dc in range(DC):
                    r.op("tensor", "matmul", PS_A[:, st * 512:(st + 1) * 512],
                         wq_sb[:, dc * 1024 + p * 128: dc * 1024 + p * 128 + 128],
                         xn[:, dc * OWN + qo + st * 512: dc * OWN + qo + st * 512 + 512],
                         start=(dc == 0), stop=(dc == DC - 1))
            r.inc_last("tensor", T)
            return emit_qrope_vec(T.v, Qpair)

        v_qpair, v_psA_free0 = emit_qbuild(0)
        sA63_prev = 0
        sB63_prev = 0
        for p in range(8):
            hA, hB = PAIRS[p]
            Ksrc = K01 if p < 4 else K23
            VA = Vt[hA // 4]
            VB = Vt[hB // 4]
            sA = {}
            sB = {}
            t_pv = {}
            keepA = KEEP[hA]
            keepB = KEEP[hB]
            kcs = keepB
            r.wait("tensor", V, v_qpair)
            if sB63_prev:
                r.wait("tensor", S, sB63_prev)

            prefix = kcs.index(keepA[0])
            do_prebuild = (p < 7 and prefix >= 3)
            if do_prebuild:
                pre_mms = [(st, dc) for st in range(2) for dc in range(DC)]
                n_slots = max(1, prefix - 2)
                per_slot = -(-len(pre_mms) // n_slots)
            v_qnext = 0
            v_psA_next = 0

            def emit_pv(it):
                kc = kcs[it]
                buf = (it % 3) * 1024
                if it == 0 and v_pvB_free:
                    r.wait("tensor", V, v_pvB_free)
                if kc == keepA[0] and v_pvA_free:
                    r.wait("tensor", V, v_pvA_free)
                r.wait("tensor", S, sB[kc])
                if kc in sA:
                    for st in range(2):
                        r.op("tensor", "matmul", PVA[0:65, st * 512:(st + 1) * 512],
                             VA[:, kc * 65: kc * 65 + 65],
                             PT_A[:, buf + st * 512: buf + st * 512 + 512],
                             start=(kc == keepA[0]), stop=(kc == 63),
                             skip_group_check=True)
                for st in range(2):
                    r.op("tensor", "matmul", PVB[0:65, st * 512:(st + 1) * 512],
                         VB[:, kc * 65: kc * 65 + 65],
                         PT_B[:, buf + st * 512: buf + st * 512 + 512],
                         start=(kc == keepB[0]), stop=(kc == 63),
                         skip_group_check=True)
                r.inc_last("tensor", T)
                t_pv[it] = T.v

            prevA = None
            for it, kc in enumerate(kcs):
                doA = kc in keepA
                if doA:
                    if prevA is not None:
                        r.wait("tensor", S, sA[prevA])
                    elif v_psA_next:
                        r.wait("tensor", V, v_psA_next)
                    elif p == 0:
                        r.wait("tensor", V, v_psA_free0)
                    elif sA63_prev:
                        r.wait("tensor", S, sA63_prev)
                    for st in range(2):
                        r.op("tensor", "matmul", PS_A[:, st * 512:(st + 1) * 512],
                             Ksrc[0:64, kc * 128:(kc + 1) * 128],
                             Qpair[0:64, st * 512:(st + 1) * 512], start=True, stop=True)
                    r.inc_last("tensor", T)
                    t_qkA = T.v
                if it > 0:
                    r.wait("tensor", S, sB[kcs[it - 1]])
                for st in range(2):
                    r.op("tensor", "matmul", PS_B[:, st * 512:(st + 1) * 512],
                         Ksrc[64:128, kc * 128:(kc + 1) * 128],
                         Qpair[64:128, st * 512:(st + 1) * 512], start=True, stop=True)
                r.inc_last("tensor", T)
                t_qkB = T.v
                buf = (it % 3) * 1024
                if doA:
                    r.wait("scalar", T, t_qkA)
                    if it >= 3:
                        r.wait("scalar", T, t_pv[it - 3])
                    elif t_pv_prev:
                        r.wait("scalar", T, t_pv_prev)
                    r.op("scalar", "activation", PT_A[:, buf:buf + 1024], PS_A[:, :], AF.Exp,
                         bias=alibi_sb[:, hA * 64 + kc: hA * 64 + kc + 1], inc=[(S, 1)])
                    sA[kc] = S.v
                    prevA = kc
                r.wait("scalar", T, t_qkB)
                if it >= 3:
                    r.wait("scalar", T, t_pv[it - 3])
                elif t_pv_prev:
                    r.wait("scalar", T, t_pv_prev)
                r.op("scalar", "activation", PT_B[:, buf:buf + 1024], PS_B[:, :], AF.Exp,
                     bias=alibi_sb[:, hB * 64 + kc: hB * 64 + kc + 1], inc=[(S, 1)])
                sB[kc] = S.v
                if do_prebuild and it < n_slots and v_qnext == 0:
                    chunk = pre_mms[it * per_slot:(it + 1) * per_slot]
                    for j, (st, dc) in enumerate(chunk):
                        if it == 0 and j == 0:
                            if p == 0:
                                r.wait("tensor", V, v_psA_free0)
                            elif sA63_prev:
                                r.wait("tensor", S, sA63_prev)
                        r.op("tensor", "matmul", PS_A[:, st * 512:(st + 1) * 512],
                             wq_sb[:, dc * 1024 + (p + 1) * 128:
                                   dc * 1024 + (p + 1) * 128 + 128],
                             xn[:, dc * OWN + qo + st * 512:
                                dc * OWN + qo + st * 512 + 512],
                             start=(dc == 0), stop=(dc == DC - 1),
                             skip_group_check=True)
                    if chunk and (it + 1) * per_slot >= len(pre_mms):
                        r.inc_last("tensor", T)
                        v_qnext, v_psA_next = emit_qrope_vec(T.v, Qnext)
                if it >= 1:
                    emit_pv(it - 1)
            emit_pv(len(kcs) - 1)
            if do_prebuild:
                r.wait("vector", T, t_qkB)
                r.op("vector", "tensor_copy", Qpair[:, :], Qnext[:, :], inc=[(V, 1)])
                v_qpair = V.v
            r.wait("scalar", T, t_pv[len(kcs) - 1])
            r.op("scalar", "activation", rtmp[64:65, 0:1024], PVB[64:65, :], AF.Ln)
            r.chain("scalar")
            r.op("scalar", "activation", rlb[64:65, 1024:2048], rtmp[64:65, 0:1024],
                 AF.Exp, scale=-1.0, inc=[(S, 1)])
            s_recB = S.v
            r.chain("scalar")
            r.op("scalar", "activation", rtmp[64:65, 0:1024], PVA[64:65, :], AF.Ln)
            r.chain("scalar")
            r.op("scalar", "activation", rlb[64:65, 0:1024], rtmp[64:65, 0:1024],
                 AF.Exp, scale=-1.0, inc=[(S, 1)])
            s_recA = S.v
            if p < 7 and not do_prebuild:
                v_qpair, v_psA_next = emit_qbuild(p + 1, gate_s=sB[63], gate_t=t_qkB)
            for head, PVx, hh, s_rec in ((1, PVB, hB, s_recB), (0, PVA, hA, s_recA)):
                r.wait("tensor", S, s_rec)
                for st in range(2):
                    r.op("tensor", "matmul", PVx[64:128, st * 512:(st + 1) * 512],
                         ones64[64:65, 0:64],
                         rlb[64:65, head * 1024 + st * 512: head * 1024 + (st + 1) * 512],
                         start=True, stop=True, skip_group_check=True)
                r.inc_last("tensor", T)
                t_bc = T.v
                r.wait("vector", T, t_bc)
                ro = (hh % 2) * 64
                co = (hh // 2) * 1024
                for st in range(2):
                    r.op("vector", "tensor_copy", rb_sb[:, :],
                         PVx[64:128, st * 512:(st + 1) * 512])
                    r.op("vector", "tensor_mul",
                         prewo[ro:ro + 64, co + st * 512:co + (st + 1) * 512],
                         PVx[0:64, st * 512:(st + 1) * 512], rb_sb[:, :],
                         inc=[(V, 1)])
                if head == 1:
                    v_pvB_free = V.v
                else:
                    v_pvA_free = V.v
            v_prewo_done = V.v
            t_pv_prev = t_pv[len(kcs) - 1]
            sA63_prev = sA[63]
            sB63_prev = sB[63]

        r.wait("tensor", V, v_prewo_done)
        for oc in range(DC):
            PSx = PS_A if (oc % 2 == 0) else PS_B
            for st in range(2):
                for ic in range(DC):
                    r.op("tensor", "matmul", PSx[:, st * 512:(st + 1) * 512],
                         wo_sb[:, ic * 1024 + oc * 128: ic * 1024 + oc * 128 + 128],
                         prewo[:, ic * 1024 + st * 512: ic * 1024 + st * 512 + 512],
                         start=(ic == 0), stop=(ic == DC - 1))
            r.inc_last("tensor", T)
            t_wo = T.v
            zb = (oc % 2) * 1024
            r.wait("vector", T, t_wo)
            szp = sz[z_writes % 2]
            if z_writes >= 2:
                r.wait("vector", sz[z_writes % 2], ds_zw[z_writes - 2])
            r.op("vector", "tensor_copy", zst[:, zb:zb + 1024], PSx[:, :], inc=[(V, 1)])
            v_ev = V.v
            r.wait("sync", V, v_ev)
            r.op("sync", "dma_start", z_d[oc, :, qo:qo + 1024], zst[:, zb:zb + 1024],
                 inc=[(szp, 16)])
            ds_zw[z_writes] = szp.v
            z_writes += 1
            if oc + 2 < DC:
                r.wait("tensor", V, v_ev)
        v_prewo_done = V.v
    sz_fin = (sz[0].v, sz[1].v)

    if phases <= 2:
        _dbg_z(r, nc, out_d, z_d, sdbg, (T, V, S), all_dma)
    r.emit(nc)
    b_sc.close()
    ab.close()
    if phases <= 2:
        top.close(); _lp.__exit__(None, None, None)
        return nc

    e_sc = ExitStack()
    zacc = e_sc.enter_context(nc.sbuf_tensor("zfsb", [128, 16384], F32))
    c_sc = ExitStack()
    sel_sb = c_sc.enter_context(nc.sbuf_tensor("sel_sb", [8, 1024], BF16))
    w1sb = c_sc.enter_context(nc.sbuf_tensor("w1sb", [128, 2 * 12288], BF16))
    w2sb = c_sc.enter_context(nc.sbuf_tensor("w2sb", [128, 2 * 6144], BF16))
    Abuf = c_sc.enter_context(nc.sbuf_tensor("Abuf", [128, 6 * 2048], BF16))
    gact = c_sc.enter_context(nc.sbuf_tensor("gact", [128, 1024], F32))
    wbc = c_sc.enter_context(nc.sbuf_tensor("wbc", [128, 2048], BF16))
    PA1 = c_sc.enter_context(nc.psum_tensor("PA1", [128, 1024], F32))
    PA2 = c_sc.enter_context(nc.psum_tensor("PA2", [128, 1024], F32))
    PB1 = c_sc.enter_context(nc.psum_tensor("PB1", [128, 1024], F32))
    PB2 = c_sc.enter_context(nc.psum_tensor("PB2", [128, 1024], F32))

    r = Rec(SELFS)
    r.auto_chain("vector", True)
    r.auto_chain("scalar", True)
    _load_table(r, SIL)
    r.op("gpsimd", "dma_start", sel_sb[:, :], sel_d, inc=[(gset, 16)])
    gset_sel = gset.v
    r.wait("gpsimd", sz[0], sz_fin[0])
    r.wait("gpsimd", sz[1], sz_fin[1])

    dg_w = {}
    for e in range(2):
        buf = e % 2
        r.op("gpsimd", "dma_start",
             w1sb[:, buf * 12288:(buf + 1) * 12288].rearrange("p (dc f) -> p dc f", dc=DC),
             w1_d[e].rearrange("(dc p) f -> p dc f", p=128), inc=[(gw[buf], 16)])
        r.op("gpsimd", "dma_start",
             w2sb[:, buf * 6144:(buf + 1) * 6144].rearrange("p (hc c) -> p hc c", hc=6),
             w2_d[e].rearrange("(hc p) c -> p hc c", p=128), inc=[(gw[buf], 16)])
        dg_w[e] = gw[buf].v

    v_A_done = {}
    v_z3evac = {}
    dg_acc = {}
    t_h1_done = {}
    t_eo_done = {}
    v_last_eoevac = 0
    v_gact_rd = 0
    for e in range(NE):
        buf = e % 2
        r.wait("tensor", gw[buf], dg_w[e])
        if v_last_eoevac:
            r.wait("tensor", V, v_last_eoevac)
        if e < 8:
            if e == 0:
                r.wait("tensor", gset, gset_sel)
            for st in range(4):
                Px = (PA1, PA2, PB1, PB2)[st]
                r.op("tensor", "matmul", Px[:, 0:512],
                     sel_sb[0:8, e * 128:(e + 1) * 128], wfmb[0:8, st * 512:(st + 1) * 512],
                     start=True, stop=True)
            r.inc_last("tensor", T)
            t_wb = T.v
            r.wait("vector", T, t_wb)
            for st in range(4):
                Px = (PA1, PA2, PB1, PB2)[st]
                r.op("vector", "tensor_copy", wbc[:, st * 512:(st + 1) * 512],
                     Px[:, 0:512], inc=[(V, 1)])
            v_wbc = V.v
            r.wait("tensor", V, v_wbc)
        for j in range(6):
            for half in range(2):
                P1 = PA1 if half == 0 else PB1
                P2 = PA2 if half == 0 else PB2
                if (j - 1, half) in v_A_done:
                    r.wait("tensor", V, v_A_done[(j - 1, half)])
                for fcj, Px in ((j, P1), (j + 6, P2)):
                    for st in range(2):
                        for dc in range(DC):
                            r.op("tensor", "matmul", Px[:, st * 512:(st + 1) * 512],
                                 w1sb[:, buf * 12288 + dc * 1536 + fcj * 128:
                                      buf * 12288 + dc * 1536 + fcj * 128 + 128],
                                 xn[:, dc * OWN + half * 1024 + st * 512:
                                    dc * OWN + half * 1024 + st * 512 + 512],
                                 start=(dc == 0), stop=(dc == DC - 1))
                r.inc_last("tensor", T)
                t_h1 = T.v
                r.wait("scalar", T, t_h1)
                if v_gact_rd:
                    r.wait("scalar", V, v_gact_rd)
                r.op("scalar", "activation", gact[:, :],
                     P2[:, :], AF.Silu, inc=[(S, 1)])
                s_si = S.v
                r.wait("vector", S, s_si)
                acol = j * 2048 + half * 1024
                if e < 8:
                    r.op("vector", "tensor_mul", Abuf[:, acol:acol + 1024], P1[:, :],
                         gact[:, :], inc=[(V, 1)])
                    v_gact_rd = V.v
                    r.op("vector", "tensor_mul", Abuf[:, acol:acol + 1024],
                         Abuf[:, acol:acol + 1024],
                         wbc[:, half * 1024:(half + 1) * 1024], inc=[(V, 1)])
                else:
                    r.op("vector", "tensor_mul", Abuf[:, acol:acol + 1024], P1[:, :],
                         gact[:, :], inc=[(V, 1)])
                    v_gact_rd = V.v
                v_A_done[(j, half)] = V.v
        v_A_done.pop((-1, 0), None)
        t_h1_done[e] = T.v
        if e + 2 < NE:
            r.wait("gpsimd", T, t_h1_done[e])
            r.op("gpsimd", "dma_start",
                 w1sb[:, buf * 12288:(buf + 1) * 12288].rearrange("p (dc f) -> p dc f", dc=DC),
                 w1_d[e + 2].rearrange("(dc p) f -> p dc f", p=128), inc=[(gw[buf], 16)])
        r.wait("tensor", V, v_A_done[(5, 1)])
        eo_idx = 0
        v_eo_evac = {}
        for dc in range(DC):
            for half in range(2):
                P1 = PA1 if (eo_idx % 2 == 0) else PB1
                if eo_idx >= 2:
                    r.wait("tensor", V, v_eo_evac[eo_idx - 2])
                for hc in range(6):
                    for st in range(2):
                        r.op("tensor", "matmul", P1[:, st * 512:(st + 1) * 512],
                             w2sb[:, buf * 6144 + hc * 1024 + dc * 128:
                                  buf * 6144 + hc * 1024 + dc * 128 + 128],
                             Abuf[:, hc * 2048 + half * 1024 + st * 512:
                                  hc * 2048 + half * 1024 + st * 512 + 512],
                             start=(hc == 0), stop=(hc == 5))
                r.inc_last("tensor", T)
                t_eo = T.v
                r.wait("vector", T, t_eo)
                zcol = dc * 2048 + half * 1024
                if e == 0:
                    r.op("vector", "tensor_copy", zacc[:, zcol:zcol + 1024],
                         P1[:, :], inc=[(V, 1)])
                else:
                    r.op("vector", "tensor_add", zacc[:, zcol:zcol + 1024],
                         zacc[:, zcol:zcol + 1024], P1[:, :], inc=[(V, 1)])
                v_eo_evac[eo_idx] = V.v
                eo_idx += 1
        t_eo_done[e] = T.v
        v_last_eoevac = V.v
        if e + 2 < NE:
            r.wait("gpsimd", T, t_eo_done[e])
            r.op("gpsimd", "dma_start",
                 w2sb[:, buf * 6144:(buf + 1) * 6144].rearrange("p (hc c) -> p hc c", hc=6),
                 w2_d[e + 2].rearrange("(hc p) c -> p hc c", p=128), inc=[(gw[buf], 16)])
            dg_w[e + 2] = gw[buf].v
        v_A_done = {(5, 0): v_A_done[(5, 0)], (5, 1): v_A_done[(5, 1)]}
        v_A_done = {(-1, 0): v_A_done[(5, 0)], (-1, 1): v_A_done[(5, 1)]}

    if phases <= 3:
        _dbg_z(r, nc, out_d, z_d, sdbg, (T, V, S), all_dma)
    r.emit(nc)
    c_sc.close()
    if phases <= 3:
        e_sc.close()
        top.close(); _lp.__exit__(None, None, None)
        return nc

    d_sc = ExitStack()
    zin = d_sc.enter_context(nc.sbuf_tensor("zin", [128, 2048], F32))
    xtb = d_sc.enter_context(nc.sbuf_tensor("xtb", [128, 2048], F32))
    zhi = d_sc.enter_context(nc.sbuf_tensor("zhi", [128, 2048], BF16))
    zhi32 = d_sc.enter_context(nc.sbuf_tensor("zhi32", [128, 2048], F32))
    zlo = d_sc.enter_context(nc.sbuf_tensor("zlo", [128, 2048], BF16))
    osb = d_sc.enter_context(nc.sbuf_tensor("osb", [128, 2048], F32))
    PO1 = d_sc.enter_context(nc.psum_tensor("PO1", [128, 1024], F32))
    PO2 = d_sc.enter_context(nc.psum_tensor("PO2", [128, 1024], F32))

    r = Rec(SELFS)
    r.auto_chain("vector", True)
    z_r = z_d.rearrange("dc p t -> p dc t")
    v_used = {}
    t_tr4 = {}
    ds_out = {}
    for tb in range(16):
        bb = (tb % 2) * 1024
        s4p = s4[tb % 2]
        if tb >= 2:
            r.wait("sync", V, v_used[tb - 2])
        r.op("sync", "dma_start", zin[:, bb:bb + 1024].rearrange("p (dc c) -> p dc c", dc=DC),
             z_r[:, :, tb * 128:(tb + 1) * 128], inc=[(s4p, 16)])
        r.op("sync", "dma_start", xtb[:, bb:bb + 1024], xo_r[:, tb, :], inc=[(s4p, 16)])
        ds_in = s4p.v
        r.wait("vector", s4p, ds_in)
        if tb >= 2:
            r.wait("vector", T, t_tr4[tb - 2])
            r.wait("vector", so[tb % 2], ds_out[tb - 2])
        r.op("vector", "tensor_add",
             zin[:, bb:bb + 1024].rearrange("p (dc c) -> p dc c", dc=DC),
             zin[:, bb:bb + 1024].rearrange("p (dc c) -> p dc c", dc=DC),
             zacc[:, :].rearrange("p (dc t) -> p dc t", dc=DC)[:, :, tb * 128:(tb + 1) * 128])
        r.op("vector", "tensor_copy", zhi[:, bb:bb + 1024], zin[:, bb:bb + 1024])
        r.op("vector", "tensor_copy", zhi32[:, bb:bb + 1024], zhi[:, bb:bb + 1024])
        r.op("vector", "tensor_sub", zlo[:, bb:bb + 1024], zin[:, bb:bb + 1024],
             zhi32[:, bb:bb + 1024], inc=[(V, 1)])
        v_hl = V.v
        POx = PO1 if tb % 2 == 0 else PO2
        r.wait("tensor", V, v_hl)
        for dc in range(DC):
            for srci, srcm in ((0, zhi), (1, zlo)):
                r.op("tensor", "matmul", POx[:, dc * 128:(dc + 1) * 128],
                     srcm[:, bb + dc * 128: bb + (dc + 1) * 128], id_sb[:, :],
                     start=(dc % 4 == 0 and srci == 0), stop=(dc % 4 == 3 and srci == 1),
                     skip_group_check=True)
        r.inc_last("tensor", T)
        t_tr4[tb] = T.v
        r.wait("vector", T, T.v)
        r.op("vector", "tensor_add", osb[:, bb:bb + 1024], xtb[:, bb:bb + 1024], POx[:, :],
             inc=[(V, 1)])
        v_used[tb] = V.v
        r.wait("sync", V, V.v)
        r.op("sync", "dma_start", out_d[tb * 128:(tb + 1) * 128, :], osb[:, bb:bb + 1024],
             inc=[(so[tb % 2], 16)])
        ds_out[tb] = so[tb % 2].v
    for ctr in all_dma:
        r.wait("sync", ctr, ctr.v)
    r.emit(nc)
    d_sc.close()
    e_sc.close()
    top.close()
    _lp.__exit__(None, None, None)
    return nc


def _dbg_z(r, nc, out_d, z_d, sdbg, ctrs, all_dma):
    for ctr in ctrs:
        r.wait("gpsimd", ctr, ctr.v)
    for ctr in all_dma:
        r.wait("gpsimd", ctr, ctr.v)
    r.op("gpsimd", "dma_start", out_d.rearrange("(dc p) t -> dc p t", p=128),
         z_d, inc=[(sdbg, 16)])
    r.wait("gpsimd", sdbg, sdbg.v)


def _dbg_out(r, nc, out_d, xn, sdbg, ctrs, all_dma):
    for ctr in ctrs:
        r.wait("gpsimd", ctr, ctr.v)
    r.op("gpsimd", "dma_start", out_d.rearrange("(a p) c -> p a c", p=128),
         xn[:, :].rearrange("p (a c) -> p a c", a=16), inc=[(sdbg, 16)])
    for ctr in all_dma:
        r.wait("gpsimd", ctr, ctr.v)


_NC = None


def _host_tables(qs):
    pos = np.arange(L)
    invf = (1.0 / ROPE_BASE ** (np.arange(0, HD, 2) / HD)).astype(np.float64)
    ang = pos[:, None].astype(np.float64) * invf[None, :]
    cos_t = np.cos(ang).astype(np.float32)
    sin_t = np.sin(ang).astype(np.float32)
    cos64 = np.concatenate([cos_t.T, cos_t.T], axis=0)
    sin64 = np.concatenate([-sin_t.T, sin_t.T], axis=0)
    cosk = np.concatenate([cos64, cos64], axis=0)
    sink = np.concatenate([sin64, sin64], axis=0)
    cosq = cosk[:, qs * OWN:(qs + 1) * OWN] * 0.125
    sinq = sink[:, qs * OWN:(qs + 1) * OWN] * 0.125
    slopes = np.asarray(_alibi_slopes(H), np.float32)
    alibi = np.zeros((128, 1024), np.float32)
    for h in range(H):
        for kc in range(64):
            j = pos[kc * 128:(kc + 1) * 128].astype(np.float32)
            alibi[:, h * 64 + kc] = slopes[h] * (j - (L - 1))
    return (cosk.astype(ml_dtypes.bfloat16), sink.astype(ml_dtypes.bfloat16),
            cosq.astype(ml_dtypes.bfloat16), sinq.astype(ml_dtypes.bfloat16), alibi)


def prep_in_maps(L0_attn_x, L0_attn_Wq, L0_attn_Wk, L0_attn_Wv, L0_attn_Wo,
                 L0_ffn_gate_w, L0_ffn_W1, L0_ffn_W2, L0_ffn_Ws1, L0_ffn_Ws2):
    x = np.ascontiguousarray(np.asarray(L0_attn_x, np.float32))
    Wq = np.asarray(L0_attn_Wq, np.float32)
    Wk = np.ascontiguousarray(np.asarray(L0_attn_Wk, np.float32))
    Wv = np.ascontiguousarray(np.asarray(L0_attn_Wv, np.float32))
    Wo = np.ascontiguousarray(np.asarray(L0_attn_Wo, np.float32))
    gw = np.asarray(L0_ffn_gate_w, np.float32)
    W1 = np.asarray(L0_ffn_W1, np.float32)
    W2 = np.asarray(L0_ffn_W2, np.float32)
    Ws1 = np.asarray(L0_ffn_Ws1, np.float32)
    Ws2 = np.asarray(L0_ffn_Ws2, np.float32)

    colperm = np.concatenate([np.arange(h * 64, (h + 1) * 64)
                              for pr in PAIRS for h in pr])
    wqsh = np.ascontiguousarray(Wq[:, colperm])
    gt = np.ascontiguousarray(gw.T)
    w1t = np.empty((NE, D, 1536), np.float32)
    w2t = np.empty((NE, 768, D), np.float32)
    for e in range(8):
        w1t[e] = W1[e].T
        w2t[e] = W2[e].T
    for sh in range(2):
        w1t[8 + sh] = np.concatenate([Ws1[sh * 768:(sh + 1) * 768],
                                      Ws1[1536 + sh * 768: 1536 + (sh + 1) * 768]],
                                     axis=0).T
        w2t[8 + sh] = Ws2.T[sh * 768:(sh + 1) * 768]
    w1t = np.ascontiguousarray(w1t)
    w2t = np.ascontiguousarray(w2t)
    ident = np.eye(128, dtype=np.float32).astype(ml_dtypes.bfloat16)
    sel = np.zeros((8, 1024), np.float32)
    for e in range(8):
        sel[e, e * 128:(e + 1) * 128] = 1.0
    sel = sel.astype(ml_dtypes.bfloat16)

    in_maps = []
    for c in range(8):
        b, qs = c // 4, c % 4
        cosk, sink, cosq, sinq, alibi = _host_tables(qs)
        xb = np.ascontiguousarray(x[b])
        xown = np.ascontiguousarray(x[b, qs * OWN:(qs + 1) * OWN])
        in_maps.append({
            "xb": xb, "xown": xown, "wqsh": wqsh, "wk": Wk, "wv": Wv, "wo": Wo, "gt": gt,
            "w1t": w1t, "w2t": w2t, "cosk": cosk, "sink": sink, "cosq": cosq,
            "sinq": sinq, "alibi": alibi, "ident": ident, "sel": sel,
        })
    return in_maps


_NCS = {}


def kernel(**inputs):
    import os
    phases = int(os.environ.get("KMK_PHASES", "4"))
    nonce0 = int(os.environ.get("KMK_NONCE", "0"))
    in_maps = prep_in_maps(**inputs)
    ncores = int(os.environ.get("KMK_CORES", "8"))
    in_maps = in_maps[:ncores]
    res = None
    last = None
    for k in range(nonce0, nonce0 + 6):
        key = (phases, k)
        if key not in _NCS:
            _NCS[key] = build_nc(phases=phases, nonce=k,
                                 stripes=int(os.environ.get("KMK_STRIPES", "16")))
        try:
            res = run_bass_kernel_spmd(_NCS[key], in_maps, core_ids=list(range(ncores)))
            break
        except Exception as e:
            last = e
            print(f"[kernel] launch attempt nonce={k} failed: {type(e).__name__}; retrying", flush=True)
    if res is None:
        raise last
    out = np.zeros((B, L, D), np.float32)
    for c in range(ncores):
        b, qs = c // 4, c % 4
        out[b, qs * OWN:(qs + 1) * OWN, :] = res.results[c]["out"]
    return out
